# version 06
# baseline (speedup 1.0000x reference)
import contextlib
import os
os.environ.setdefault("NEURON_SCRATCHPAD_PAGE_SIZE", "512")
import numpy as np

import concourse.bass as bass
import concourse.mybir as mybir
from concourse.bass_utils import run_bass_kernel_spmd

F32 = mybir.dt.float32
BF16 = mybir.dt.bfloat16
U32 = mybir.dt.uint32
AF = mybir.ActivationFunctionType
ALU = mybir.AluOpType
AX = mybir.AxisListType

DEBUG = False

NCORES = 8
BC = 2
L = 2048
C = 1024
H = 8
DH = 128
TK = BC * L
NT = TK // 128
JC = C // 128
E = 8
FF = 4096
FCN = FF // 128
CAP = 10240
SPC = CAP // NCORES
RPR = 16384
DTR = NCORES * RPR
YR = 10752
EPS_ATT = 1e-6
EPS_LN = 1e-5

ENGS = ("sync", "scalar", "vector", "tensor", "gpsimd")


class S:
    def __init__(self, h):
        self.h = h
        self.n = 0


class DS:
    def __init__(self, sems):
        self.sems = sems
        self.i = 0

    def nxt(self):
        s = self.sems[self.i % len(self.sems)]
        self.i += 1
        return s

    def mark(self):
        return [s.n for s in self.sems]


class EM:
    def __init__(self):
        self.q = {k: [] for k in ENGS}
        self.auto = {}

    def op(self, eng, fn, inc=None, k=1):
        s_auto = self.auto.get(eng)
        if s_auto is not None and (inc is None or inc is s_auto):
            self.q[eng].append(("wait", s_auto, s_auto.n, 0))
            inc, k = s_auto, 1
        self.q[eng].append(("op", fn, inc, k))
        if inc is not None:
            inc.n += k

    def wait(self, eng, sem, val=None):
        v = sem.n if val is None else val
        self.q[eng].append(("wait", sem, v, 0))

    def waitm(self, eng, grp, mark=None):
        if mark is None:
            mark = grp.mark()
        for s, v in zip(grp.sems, mark):
            if v > 0:
                self.wait(eng, s, v)

    def replay(self, block):
        def make(name):
            items = self.q[name]
            def body(e):
                for it in items:
                    if it[0] == "wait":
                        e.wait_ge(it[1].h, it[2])
                    else:
                        ins = it[1](e)
                        if it[2] is not None:
                            ins.then_inc(it[2].h, it[3])
            return body
        for name in ENGS:
            getattr(block, name)(make(name))


def build():
    nc = bass.Bass("TRN2", target_bir_lowering=False)

    okind = "ExternalOutput" if DEBUG else "Internal"
    x_h = nc.dram_tensor("x", [TK, C], F32, kind="ExternalInput").ap()
    wq_h = nc.dram_tensor("wq", [C, C], F32, kind="ExternalInput").ap()
    wk_h = nc.dram_tensor("wk", [C, C], F32, kind="ExternalInput").ap()
    wv_h = nc.dram_tensor("wv", [C, C], F32, kind="ExternalInput").ap()
    bqT_h = nc.dram_tensor("bqT_d", [128, JC], F32, kind="ExternalInput").ap()
    bk_h = nc.dram_tensor("bk", [1, C], F32, kind="ExternalInput").ap()
    bv_h = nc.dram_tensor("bv", [1, C], F32, kind="ExternalInput").ap()
    wrT_h = nc.dram_tensor("wrT_d", [128, JC * E], F32, kind="ExternalInput").ap()
    b1T_h = nc.dram_tensor("b1T_d", [128, E * FCN], F32, kind="ExternalInput").ap()
    w1_h = nc.dram_tensor("w1", [E, C, FF], F32, kind="ExternalInput").ap()
    w2_h = nc.dram_tensor("w2", [E, FF, C], F32, kind="ExternalInput").ap()
    ident_h = nc.dram_tensor("ident_d", [128, 128], F32, kind="ExternalInput").ap()
    triu_h = nc.dram_tensor("triu_d", [128, 128], F32, kind="ExternalInput").ap()
    onesrow_h = nc.dram_tensor("onesrow_d", [1, 128], F32, kind="ExternalInput").ap()
    onesrowb_h = nc.dram_tensor("onesrowb_d", [1, 128], F32, kind="ExternalInput").ap()
    onescol_h = nc.dram_tensor("onescol_d", [128, 1], F32, kind="ExternalInput").ap()
    cmask_h = nc.dram_tensor("cmask_d", [E, 1], F32, kind="ExternalInput").ap()
    tgrid_h = nc.dram_tensor("tgrid_d", [128, NT], F32, kind="ExternalInput").ap()
    iota8b_h = nc.dram_tensor("iota8b_d", [128, E], F32, kind="ExternalInput").ap()
    pcol_h = nc.dram_tensor("pcol_d", [128, 1], F32, kind="ExternalInput").ap()
    trow_h = nc.dram_tensor("trow_d", [128, 1], F32, kind="ExternalInput").ap()
    cgrid_h = nc.dram_tensor("cgrid_d", [128, 1], F32, kind="ExternalInput").ap()
    b2row_h = nc.dram_tensor("b2row_d", [1, E * C], F32, kind="ExternalInput").ap()

    y_h = nc.dram_tensor("y", [TK, C], F32, kind="ExternalOutput").ap()
    xtok_h = nc.dram_tensor("xtok", [TK, C], F32).ap()
    xall_h = nc.dram_tensor("xall", [NCORES * TK, C], F32).ap()
    dtab_h = nc.dram_tensor("dtab", [DTR, 2], F32).ap()
    dtag_h = nc.dram_tensor("dtag", [NCORES * DTR, 2], F32).ap()
    yslots_h = nc.dram_tensor("yslots", [YR, C], F32).ap()
    ytall_h = nc.dram_tensor("ytall", [NCORES * YR, C], F32).ap()
    if DEBUG:
        xnT_h = nc.dram_tensor("xnT_s", [128, JC, TK], BF16, kind=okind).ap()
        attn_h = nc.dram_tensor("attn_s", [TK, C], F32, kind=okind).ap()
        dt_dbg_h = nc.dram_tensor("dt_dbg", [128, 256], F32, kind="ExternalOutput").ap()
        ys_dbg_h = nc.dram_tensor("ys_dbg", [YR, C], F32, kind="ExternalOutput").ap()
    else:
        xnT_h = nc.dram_tensor("xnT_s", [128, JC, TK], BF16).ap()
        attn_h = nc.dram_tensor("attn_s", [TK, C], F32).ap()
    cci_h = nc.dram_tensor("cc_in", [1, E], F32).ap()
    cco_h = nc.dram_tensor("cc_out", [NCORES, E], F32).ap()
    if DEBUG:
        gates_dbg_h = nc.dram_tensor("gates_dbg", [128, NT * E], F32, kind="ExternalOutput").ap()
        pos_dbg_h = nc.dram_tensor("pos_dbg", [128, NT * E], F32, kind="ExternalOutput").ap()
        h_dbg_h = nc.dram_tensor("h_dbg", [128, FCN * 512], BF16, kind="ExternalOutput").ap()
        x4_dbg_h = nc.dram_tensor("x4_dbg", [128, JC * 512], BF16, kind="ExternalOutput").ap()
        w1_dbg_h = nc.dram_tensor("w1_dbg", [128, JC * 1024], BF16, kind="ExternalOutput").ap()
        m_dbg_h = nc.dram_tensor("m_dbg", [128, 512], F32, kind="ExternalOutput").ap()
        pre_dbg_h = nc.dram_tensor("pre_dbg", [128, 512], F32, kind="ExternalOutput").ap()

    es = contextlib.ExitStack()
    with es:
        def sb(name, shape, dt):
            return es.enter_context(nc.sbuf_tensor(name, shape, dt))

        def sem(name):
            return S(es.enter_context(nc.semaphore(name)))

        ident = sb("ident", [128, 128], F32)
        triu = sb("triu", [128, 128], F32)
        onesrow = sb("onesrow", [1, 128], F32)
        onesrowb = sb("onesrowb", [1, 128], BF16)
        onescol = sb("onescol", [128, 1], F32)
        cmask = sb("cmask", [E, 1], F32)
        bqT = sb("bqT", [128, JC], F32)
        bkb = sb("bkb", [1, C], BF16)
        bvb = sb("bvb", [1, C], BF16)
        wrT = sb("wrT", [128, JC * E], F32)
        b1T = sb("b1T", [128, E * FCN], F32)
        tgrid = sb("tgrid", [128, NT], F32)
        iota8b = sb("iota8b", [128, E], F32)
        pcol = sb("pcol", [128, 1], F32)
        trow = sb("trow", [128, 1], F32)
        cgridu = sb("cgridu", [128, 1], U32)
        cgrid = sb("cgrid", [128, 1], F32)
        dslk_all = sb("dslk_all", [128, 256], F32)
        dslk_v = sb("dslk_v", [128, 256], F32)
        offrow = sb("offrow", [1, E], F32)
        capbO = sb("capbO", [128, E], F32)
        ebc1280 = sb("ebc1280", [128, E], F32)
        ridx = sb("ridx", [128, NT * 2], U32)
        xtokb = [sb("xtokb0", [128, C], F32)]

        running = sb("running", [1, E], F32)
        countsAll = sb("countsAll", [NCORES, E], F32)
        capm = sb("capm", [1, E], F32)
        capb = sb("capb", [128, E], F32)

        epsln = sb("epsln", [128, 1], F32)
        wtmp = sb("wtmp", [128, 4], F32)
        xbuf = [sb("xbuf0", [128, C], F32), sb("xbuf1", [128, C], F32)]
        PA = es.enter_context(nc.psum_tensor("PA", [128, 2048], F32))
        PB = es.enter_context(nc.psum_tensor("PB", [128, 2048], F32))

        em = EM()
        g_sync = DS([sem(f"g_sync{i}") for i in range(4)])
        g_sq = DS([sem(f"g_sq{i}") for i in range(4)])
        g_gp = DS([sem(f"g_gp{i}") for i in range(6)])
        s_cc = sem("s_cc")
        s_v = sem("s_v")
        s_s = sem("s_s")
        s_t = sem("s_t")
        em.auto = {"vector": s_v, "scalar": s_s}


        def dma(eng, grp, dst, src, accum=False):
            if eng in em.auto:
                em.wait(eng, em.auto[eng])
            s = grp.nxt()
            em.wait(eng, s)
            if accum:
                em.op(eng, lambda e, a=dst, b=src: e.dma_start(out=a, in_=b, accum_op=ALU.add),
                      inc=s, k=16)
            else:
                em.op(eng, lambda e, a=dst, b=src: e.dma_start(out=a, in_=b), inc=s, k=16)

        def dma_fn(eng, grp, fn):
            if eng in em.auto:
                em.wait(eng, em.auto[eng])
            s = grp.nxt()
            em.wait(eng, s)
            em.op(eng, fn, inc=s, k=16)

        dma("sync", g_sync, ident[:, :], ident_h)
        dma("sync", g_sync, triu[:, :], triu_h)
        dma("sync", g_sync, onesrow[:, :], onesrow_h)
        dma("sync", g_sync, onescol[:, :], onescol_h)
        dma("sync", g_sync, tgrid[:, :], tgrid_h)
        dma("sync", g_sync, iota8b[:, :], iota8b_h)
        dma("sync", g_sync, pcol[:, :], pcol_h)
        dma("sync", g_sync, trow[:, :], trow_h)
        dma("sync", g_sync, cgrid[:, :], cgrid_h)
        dma("sync", g_sync, cmask[:, :], cmask_h)
        dma("sync", g_sync, bqT[:, :], bqT_h)
        dma("sync", g_sync, wrT[:, :], wrT_h)
        dma("sync", g_sync, b1T[:, :], b1T_h)
        dma("gpsimd", g_gp, onesrowb[:, :], onesrowb_h)
        dma("gpsimd", g_gp, bkb[:, :], bk_h)
        dma("gpsimd", g_gp, bvb[:, :], bv_h)
        startup_sync = g_sync.mark()
        startup_gp = g_gp.mark()

        es_att = contextlib.ExitStack()
        def sba(name, shape, dt):
            return es_att.enter_context(nc.sbuf_tensor(name, shape, dt))
        Wq_sb = sba("Wq_sb", [128, JC * C], BF16)
        Wk_sb = sba("Wk_sb", [128, JC * C], BF16)
        Wv_sb = sba("Wv_sb", [128, JC * C], BF16)
        xnTb = sba("xnTb", [128, JC * L], BF16)
        QT = sba("QT", [128, H * L], BF16)
        Kb = sba("Kb", [128, C], BF16)
        Vs = sba("Vs", [128, H * 132], BF16)
        KVs = sba("KVs", [128, H * 132], BF16)
        KVacc = sba("KVacc", [128, H * 132], F32)
        tmka = sba("tmka", [128, C], F32)
        tmkb = sba("tmkb", [128, C], F32)
        tq1 = sba("tq1", [128, 512], F32)
        attnbuf = [sba("attnb0", [128, C], F32), sba("attnb1", [128, C], F32)]
        tq2 = sba("tq2", [128, 512], F32)
        zden = sba("zden", [128, H], F32)
        zrec = sba("zrec", [128, H], F32)

        es_r = contextlib.ExitStack()
        pos_all = es_r.enter_context(nc.sbuf_tensor("pos_all", [128, NT * E], F32))
        gates_all = es_r.enter_context(nc.sbuf_tensor("gates_all", [128, NT * E], F32))
        kept = es_r.enter_context(nc.sbuf_tensor("kept", [128, NT * E], F32))
        eq1_all = es_r.enter_context(nc.sbuf_tensor("eq1_all", [128, NT * E], F32))
        eq2p_all = es_r.enter_context(nc.sbuf_tensor("eq2p_all", [128, NT * E], F32))

        for W_sb, W_h in ((Wq_sb, wq_h), (Wk_sb, wk_h), (Wv_sb, wv_h)):
            src = W_h.rearrange("(j p) d -> p j d", p=128)
            dst = W_sb[:, :].rearrange("p (j d) -> p j d", j=JC)
            dma("gpsimd", g_gp, dst, src)
        wqkv_gp = g_gp.mark()

        es_p1 = contextlib.ExitStack()
        ptr = [PA[:, 0:1024], PA[:, 1024:2048]]
        plg = PB[:, 0:8]
        ppos = PB[:, 512:520]
        ptt = PB[0:1, 1024:1032]
        def sb1(name, shape, dt):
            return es_p1.enter_context(nc.sbuf_tensor(name, shape, dt))
        bstats = sb1("bstats", [128, 12], F32)
        gvar = sb1("gvar", [128, 4], F32)
        gmsq = sb1("gmsq", [128, 4], F32)
        ex2g = sb1("ex2g", [128, 4], F32)
        msum = sb1("msum", [128, 1], F32)
        exs = sb1("exs", [128, 1], F32)
        meanv = sb1("meanv", [128, 1], F32)
        msq2 = sb1("msq2", [128, 1], F32)
        varv = sb1("varv", [128, 1], F32)
        xn = sb1("xn", [128, C], F32)
        xnT32 = sb1("xnT32", [128, C], F32)
        xnTbf = sb1("xnTbf", [128, C], BF16)
        lgs = sb1("lgs", [128, E], F32)
        top8 = sb1("top8", [128, E], F32)
        dpair = sb1("dpair", [128, 2], F32)
        ep_ = sb1("ep", [128, 2], F32)
        ep1 = sb1("ep1", [128, 2], F32)
        gp = sb1("gp", [128, 2], F32)
        eq2 = sb1("eq2", [128, E], F32)
        t12 = sb1("t12", [128, E], F32)
        mask01 = sb1("mask01", [128, E], F32)
        m1t = sb1("m1t", [128, E], F32)
        lnv = sb1("lnv", [128, 1], F32)
        rstd = sb1("rstd", [128, 1], F32)

        em.op("vector", lambda e: e.memset(running[:, :], 0.0), inc=s_v)
        em.op("vector", lambda e: e.memset(epsln[:, :], EPS_LN), inc=s_v)
        em.wait("scalar", s_v)
        em.op("scalar", lambda e: e.activation(out=wtmp[:, 0:1], in_=epsln[:, :],
              func=AF.Gelu_apprx_tanh))
        em.op("scalar", lambda e: e.activation(out=wtmp[:, 1:2], in_=epsln[:, :], func=AF.Ln))
        em.op("scalar", lambda e: e.activation(out=wtmp[:, 2:3], in_=epsln[:, :], func=AF.Exp),
              inc=s_s)
        warm_s = s_s.n

        dma("sync", g_sync, xbuf[0][:, :], x_h[0:128, :])
        x_loaded = [g_sync.mark(), None]
        xc_done = [0, 0]

        ptr_free = [0, 0]
        xtok_store = [None]
        prev_ppos_copy = 0
        prev_plg_copy = 0
        prev_xnTbf_copy = 0
        prev_tr_done = 0
        prev_ptt_read = 0
        prev_xnT_store = None

        for t in range(NT):
            bi = t % 2
            if t + 1 < NT:
                em.wait("sync", s_v, xc_done[(t + 1) % 2])
                dma("sync", g_sync, xbuf[(t + 1) % 2][:, :],
                    x_h[(t + 1) * 128:(t + 2) * 128, :])
                x_loaded[(t + 1) % 2] = g_sync.mark()

            xt = xbuf[bi]
            em.waitm("vector", g_sync, x_loaded[bi])
            em.op("vector", lambda e, a=xt: e.bn_stats(out=bstats[:, 0:6], in_=a[:, 0:512]))
            em.op("vector", lambda e, a=xt: e.bn_stats(out=bstats[:, 6:12], in_=a[:, 512:1024]),
                  inc=s_v)
            em.wait("vector", s_v)
            bs3 = bstats[:, :].rearrange("p (g s) -> p g s", s=3)
            em.op("vector", lambda e, bs3=bs3: e.tensor_scalar(
                  out=gvar[:, :].rearrange("p (g s) -> p g s", s=1), in0=bs3[:, :, 2:3],
                  scalar1=1.0 / 256.0, scalar2=None, op0=ALU.mult))
            em.op("vector", lambda e, bs3=bs3: e.tensor_tensor(
                  out=gmsq[:, :].rearrange("p (g s) -> p g s", s=1),
                  in0=bs3[:, :, 1:2], in1=bs3[:, :, 1:2], op=ALU.mult), inc=s_v)
            em.wait("vector", s_v)
            em.op("vector", lambda e: e.tensor_tensor(out=ex2g[:, :], in0=gvar[:, :],
                  in1=gmsq[:, :], op=ALU.add))
            em.op("vector", lambda e, bs3=bs3: e.reduce_sum(
                  out=msum[:, :],
                  in_=bs3[:, :, 1:2].rearrange("p g s -> p (g s)"), axis=AX.X), inc=s_v)
            em.wait("vector", s_v)
            em.op("vector", lambda e: e.reduce_sum(out=exs[:, :], in_=ex2g[:, :], axis=AX.X))
            em.op("vector", lambda e: e.tensor_scalar(out=meanv[:, :], in0=msum[:, :],
                  scalar1=0.25, scalar2=None, op0=ALU.mult), inc=s_v)
            em.wait("vector", s_v)
            em.op("vector", lambda e: e.tensor_tensor(out=msq2[:, :], in0=meanv[:, :],
                  in1=meanv[:, :], op=ALU.mult), inc=s_v)
            em.wait("vector", s_v)
            em.op("vector", lambda e: e.scalar_tensor_tensor(out=varv[:, :], in0=exs[:, :],
                  scalar=0.25, in1=msq2[:, :], op0=ALU.mult, op1=ALU.subtract), inc=s_v)
            em.wait("scalar", s_v)
            em.wait("scalar", s_s, warm_s)
            em.op("scalar", lambda e: e.activation(out=lnv[:, :], in_=varv[:, :],
                  func=AF.Ln, bias=epsln[:, 0:1]), inc=s_s)
            em.op("scalar", lambda e: e.activation(out=rstd[:, :], in_=lnv[:, :],
                  func=AF.Exp, scale=-0.5), inc=s_s)
            em.wait("vector", s_s)
            em.wait("vector", s_t, prev_tr_done)
            em.op("vector", lambda e, a=xt: e.tensor_scalar(out=xn[:, :], in0=a[:, :],
                  scalar1=meanv[:, 0:1], scalar2=rstd[:, 0:1], op0=ALU.subtract,
                  op1=ALU.mult), inc=s_v)
            xc_done[bi] = s_v.n
            em.waitm("vector", g_gp, xtok_store[0])
            em.op("vector", lambda e: e.tensor_copy(out=xtokb[0][:, :],
                  in_=xn[:, :]), inc=s_v)
            em.wait("gpsimd", s_v)
            dma("gpsimd", g_gp, xtok_h[t * 128:(t + 1) * 128, :], xtokb[0][:, :])
            xtok_store[0] = g_gp.mark()

            em.wait("tensor", s_v)
            em.waitm("tensor", g_sync, startup_sync)
            pt = ptr[bi]
            for j in range(JC):
                em.op("tensor", lambda e, j=j, pt=pt: e.transpose(
                    out=pt[:, j * 128:(j + 1) * 128],
                    in_=xn[:, j * 128:(j + 1) * 128], identity=ident[:, :]),
                    inc=(s_t if j == JC - 1 else None))
            tr_done = s_t.n
            prev_tr_done = tr_done
            em.wait("vector", s_t, tr_done)
            em.wait("vector", s_s, prev_xnTbf_copy)
            em.op("vector", lambda e, pt=pt: e.tensor_copy(out=xnT32[:, :], in_=pt[:, :]), inc=s_v)
            ptr_free[bi] = s_v.n
            xnT32_done = s_v.n
            em.wait("scalar", s_v, xnT32_done)
            em.waitm("scalar", g_sq, prev_xnT_store)
            em.op("scalar", lambda e: e.copy(out=xnTbf[:, :], in_=xnT32[:, :]), inc=s_s)
            prev_xnTbf_copy = s_s.n
            dma_fn("scalar", g_sq, lambda e, t=t: e.dma_start(
                out=xnT_h[:, :, t * 128:(t + 1) * 128],
                in_=xnTbf[:, :].rearrange("p (j t) -> p j t", j=JC)))
            prev_xnT_store = g_sq.mark()

            em.wait("tensor", s_v, xnT32_done)
            em.wait("tensor", s_v, prev_plg_copy)
            for j in range(JC):
                em.op("tensor", lambda e, j=j: e.matmul(
                    out=plg[:, :], lhsT=xnT32[:, j * 128:(j + 1) * 128],
                    rhs=wrT[:, j * E:(j + 1) * E],
                    start=(j == 0), stop=(j == JC - 1)),
                    inc=(s_t if j == JC - 1 else None))
            plg_mm = s_t.n
            em.wait("vector", s_t, plg_mm)
            em.op("vector", lambda e: e.tensor_copy(out=lgs[:, :], in_=plg[:, :]), inc=s_v)
            prev_plg_copy = s_v.n
            em.wait("vector", s_v)

            em.op("vector", lambda e: e.max(out=top8[:, :], in_=lgs[:, :]), inc=s_v)
            em.wait("vector", s_v)
            em.op("vector", lambda e: e.tensor_tensor(out=dpair[:, 0:1], in0=top8[:, 0:1],
                  in1=top8[:, 1:2], op=ALU.subtract))
            em.op("vector", lambda e: e.tensor_tensor(out=dpair[:, 1:2], in0=top8[:, 1:2],
                  in1=top8[:, 0:1], op=ALU.subtract), inc=s_v)
            em.wait("scalar", s_v)
            em.op("scalar", lambda e: e.activation(out=ep_[:, :], in_=dpair[:, :],
                  func=AF.Exp, scale=-1.0), inc=s_s)
            em.wait("vector", s_s)
            em.op("vector", lambda e: e.tensor_scalar(out=ep1[:, :], in0=ep_[:, :],
                  scalar1=1.0, scalar2=None, op0=ALU.add), inc=s_v)
            em.wait("vector", s_v)
            em.op("vector", lambda e: e.reciprocal(out=gp[:, :], in_=ep1[:, :]), inc=s_v)
            em.wait("vector", s_v)
            sl = slice(t * E, (t + 1) * E)
            em.op("vector", lambda e, sl=sl: e.tensor_scalar(out=eq1_all[:, sl], in0=lgs[:, :],
                  scalar1=top8[:, 0:1], scalar2=None, op0=ALU.is_equal), inc=s_v)
            em.wait("vector", s_v)
            em.op("vector", lambda e: e.tensor_scalar(out=eq2[:, :], in0=lgs[:, :],
                  scalar1=top8[:, 1:2], scalar2=None, op0=ALU.is_equal), inc=s_v)
            em.wait("vector", s_v)
            em.op("vector", lambda e, sl=sl: e.tensor_tensor(out=t12[:, :], in0=eq1_all[:, sl],
                  in1=eq2[:, :], op=ALU.mult), inc=s_v)
            em.wait("vector", s_v)
            em.op("vector", lambda e, sl=sl: e.tensor_tensor(out=eq2p_all[:, sl], in0=eq2[:, :],
                  in1=t12[:, :], op=ALU.subtract), inc=s_v)
            em.wait("vector", s_v)
            em.op("vector", lambda e, sl=sl: e.tensor_tensor(out=mask01[:, :], in0=eq1_all[:, sl],
                  in1=eq2p_all[:, sl], op=ALU.add), inc=s_v)
            mask_done = s_v.n
            em.op("vector", lambda e, sl=sl: e.tensor_scalar(out=m1t[:, :], in0=eq1_all[:, sl],
                  scalar1=gp[:, 0:1], scalar2=None, op0=ALU.mult), inc=s_v)
            em.wait("vector", s_v)
            em.op("vector", lambda e, t=t, sl=sl: e.scalar_tensor_tensor(
                  out=gates_all[:, sl], in0=eq2p_all[:, sl],
                  scalar=gp[:, 1:2], in1=m1t[:, :], op0=ALU.mult, op1=ALU.add), inc=s_v)

            em.wait("tensor", s_v, mask_done)
            em.wait("tensor", s_v, prev_ppos_copy)
            em.op("tensor", lambda e: e.matmul(out=ppos[:, :], lhsT=triu[:, :],
                  rhs=mask01[:, :], start=True, stop=False))
            em.op("tensor", lambda e: e.matmul(out=ppos[:, :], lhsT=onesrow[:, :],
                  rhs=running[0:1, :], start=False, stop=True), inc=s_t)
            pos_mm = s_t.n
            em.op("tensor", lambda e: e.matmul(out=ptt[:, :], lhsT=onescol[:, :],
                  rhs=mask01[:, :], start=True, stop=True), inc=s_t)
            tt_mm = s_t.n
            em.wait("vector", s_t, pos_mm)
            em.op("vector", lambda e, t=t: e.tensor_copy(
                  out=pos_all[:, t * E:(t + 1) * E], in_=ppos[:, :]), inc=s_v)
            prev_ppos_copy = s_v.n
            em.wait("vector", s_t, tt_mm)
            em.op("vector", lambda e: e.tensor_tensor(out=running[:, :], in0=running[:, :],
                  in1=ptt[0:1, :], op=ALU.add), inc=s_v)
        running_done = s_v.n
        p1_v = s_v.n
        p1_s = s_s.n
        p1_t = s_t.n

        em.wait("gpsimd", s_v, running_done)
        dma("gpsimd", g_gp, cci_h, running[0:1, :])
        em.waitm("gpsimd", g_gp)
        em.op("gpsimd", lambda e: e.collective_compute(
            "AllGather", ALU.bypass, replica_groups=[list(range(NCORES))],
            ins=[cci_h], outs=[cco_h]), inc=s_cc)
        em.wait("gpsimd", s_cc)
        dma("gpsimd", g_gp, countsAll[:, :], cco_h)
        counts_mark = g_gp.mark()
        em.waitm("gpsimd", g_gp)
        em.op("gpsimd", lambda e: e.collective_compute(
            "AllGather", ALU.bypass, replica_groups=[list(range(NCORES))],
            ins=[xtok_h], outs=[xall_h]), inc=s_cc)
        es_p1.close()

        poff = PB[0:1, 0:8]
        pcb = PB[:, 512:520]
        pgT = [PB[0:E, 1024:1152], PB[0:E, 1536:1664]]
        em.wait("tensor", s_v)
        em.wait("tensor", s_cc, 1)
        em.waitm("tensor", g_gp, counts_mark)
        em.op("tensor", lambda e: e.matmul(out=poff[:, :], lhsT=cmask[:, :],
              rhs=countsAll[:, :], start=True, stop=True), inc=s_t)
        em.wait("vector", s_t)
        em.op("vector", lambda e: e.tensor_scalar(out=capm[:, :], in0=poff[0:1, :],
              scalar1=-1.0, scalar2=float(CAP), op0=ALU.mult, op1=ALU.add), inc=s_v)
        em.wait("tensor", s_v)
        em.op("tensor", lambda e: e.matmul(out=pcb[:, :], lhsT=onesrow[:, :],
              rhs=capm[0:1, :], start=True, stop=True), inc=s_t)
        em.wait("vector", s_t)
        em.op("vector", lambda e: e.tensor_copy(out=capb[:, :], in_=pcb[:, :]), inc=s_v)
        em.wait("vector", s_v)
        for t in range(NT):
            em.op("vector", lambda e, t=t: e.tensor_tensor(
                  out=kept[:, t * E:(t + 1) * E], in0=pos_all[:, t * E:(t + 1) * E],
                  in1=capb[:, :], op=ALU.is_lt), inc=(s_v if t == NT - 1 else None))
        em.wait("vector", s_v)
        em.op("vector", lambda e: e.tensor_tensor(out=gates_all[:, :], in0=gates_all[:, :],
              in1=kept[:, :], op=ALU.mult), inc=s_v)
        gates_final = s_v.n
        if DEBUG:
            em.wait("sync", s_v, gates_final)
            dma("sync", g_sync, gates_dbg_h, gates_all[:, :])
            dma("sync", g_sync, pos_dbg_h, pos_all[:, :])

        es_p2 = contextlib.ExitStack()
        def sb2(name, shape, dt):
            return es_p2.enter_context(nc.sbuf_tensor(name, shape, dt))
        tmpO = sb2("tmpO", [128, NT * E], F32)
        wrk = sb2("wrk", [128, NT * E], F32)
        sloto1 = sb2("sloto1", [128, NT], F32)
        sloto2 = sb2("sloto2", [128, NT], F32)
        k1a = sb2("k1a", [128, NT], F32)
        k2a = sb2("k2a", [128, NT], F32)
        g1k = sb2("g1k", [128, NT], F32)
        g2k = sb2("g2k", [128, NT], F32)
        er1 = sb2("er1", [128, NT], F32)
        er2 = sb2("er2", [128, NT], F32)
        uf = sb2("uf", [128, NT], F32)
        w32a = sb2("w32a", [128, NT], F32)
        w32b = sb2("w32b", [128, NT], F32)
        TG = sb2("TG", [128, NT * 4], F32)
        idxsc = sb2("idxsc", [128, NT * 2], U32)
        zdt = sb2("zdt", [128, DTR * 2 // 128], F32)

        em.waitm("vector", g_sq)
        em.waitm("vector", g_sync)
        em.op("vector", lambda e: e.tensor_copy(out=offrow[:, :], in_=poff[0:1, :]), inc=s_v)
        em.wait("tensor", s_v)
        em.op("tensor", lambda e: e.matmul(out=pcb[:, :], lhsT=onesrow[:, :],
              rhs=offrow[0:1, :], start=True, stop=True), inc=s_t)
        em.wait("vector", s_t)
        em.op("vector", lambda e: e.tensor_copy(out=capbO[:, :], in_=pcb[:, :]), inc=s_v)
        em.wait("vector", s_v)
        em.op("vector", lambda e: e.tensor_scalar(out=ebc1280[:, :], in0=iota8b[:, :],
              scalar1=float(SPC), scalar2=None, op0=ALU.mult), inc=s_v)
        em.wait("vector", s_v)
        for t in range(NT):
            sl = slice(t * E, (t + 1) * E)
            em.op("vector", lambda e, sl=sl: e.tensor_tensor(out=tmpO[:, sl],
                  in0=pos_all[:, sl], in1=capbO[:, :], op=ALU.add))

        def wred(dst, a, b):
            em.op("vector", lambda e, a=a, b=b: e.tensor_tensor(out=wrk[:, :], in0=a,
                  in1=b, op=ALU.mult), inc=s_v)
            em.wait("vector", s_v)
            em.op("vector", lambda e, dst=dst: e.tensor_reduce(
                  out=dst[:, :].rearrange("p (t o) -> p t o", o=1),
                  in_=wrk[:, :].rearrange("p (t e) -> p t e", e=E),
                  axis=AX.X, op=ALU.add), inc=s_v)
            em.wait("vector", s_v)

        wred(sloto1, tmpO[:, :], eq1_all[:, :])
        wred(sloto2, tmpO[:, :], eq2p_all[:, :])
        wred(k1a, kept[:, :], eq1_all[:, :])
        wred(k2a, kept[:, :], eq2p_all[:, :])
        wred(g1k, gates_all[:, :], eq1_all[:, :])
        wred(g2k, gates_all[:, :], eq2p_all[:, :])
        for dst, eqa in ((er1, eq1_all), (er2, eq2p_all)):
            for t in range(NT):
                sl = slice(t * E, (t + 1) * E)
                em.op("vector", lambda e, sl=sl, eqa=eqa: e.tensor_tensor(out=tmpO[:, sl],
                      in0=eqa[:, sl], in1=ebc1280[:, :], op=ALU.mult))
            em.op("vector", lambda e, dst=dst: e.tensor_reduce(
                  out=dst[:, :].rearrange("p (t o) -> p t o", o=1),
                  in_=tmpO[:, :].rearrange("p (t e) -> p t e", e=E),
                  axis=AX.X, op=ALU.add), inc=s_v)
            em.wait("vector", s_v)

        for a in range(2):
            so = sloto1 if a == 0 else sloto2
            ka = k1a if a == 0 else k2a
            gk = g1k if a == 0 else g2k
            er = er1 if a == 0 else er2
            em.op("vector", lambda e, so=so: e.tensor_scalar(out=uf[:, :], in0=so[:, :],
                  scalar1=float(SPC), scalar2=None, op0=ALU.is_ge), inc=s_v)
            for j in range(2, NCORES):
                em.wait("vector", s_v)
                em.op("vector", lambda e, so=so, j=j: e.tensor_scalar(out=w32b[:, :],
                      in0=so[:, :], scalar1=float(j * SPC), scalar2=None,
                      op0=ALU.is_ge), inc=s_v)
                em.wait("vector", s_v)
                em.op("vector", lambda e: e.tensor_tensor(out=uf[:, :], in0=uf[:, :],
                      in1=w32b[:, :], op=ALU.add), inc=s_v)
            em.wait("vector", s_v)
            em.op("vector", lambda e, so=so: e.tensor_scalar(out=w32a[:, :], in0=uf[:, :],
                  scalar1=float(YR - SPC), scalar2=None, op0=ALU.mult), inc=s_v)
            em.wait("vector", s_v)
            em.op("vector", lambda e, so=so: e.tensor_tensor(out=w32a[:, :], in0=w32a[:, :],
                  in1=so[:, :], op=ALU.add), inc=s_v)
            em.wait("vector", s_v)
            em.op("vector", lambda e, er=er: e.tensor_tensor(out=w32a[:, :], in0=w32a[:, :],
                  in1=er[:, :], op=ALU.add), inc=s_v)
            em.wait("vector", s_v)
            em.op("vector", lambda e, ka=ka: e.tensor_tensor(out=w32a[:, :], in0=w32a[:, :],
                  in1=ka[:, :], op=ALU.mult), inc=s_v)
            em.op("vector", lambda e, ka=ka: e.tensor_scalar(out=w32b[:, :], in0=ka[:, :],
                  scalar1=-float(YR - 1), scalar2=float(YR - 1), op0=ALU.mult,
                  op1=ALU.add), inc=s_v)
            em.wait("vector", s_v)
            em.op("vector", lambda e: e.tensor_tensor(out=w32a[:, :], in0=w32a[:, :],
                  in1=w32b[:, :], op=ALU.add), inc=s_v)
            em.wait("vector", s_v)
            em.op("vector", lambda e, a=a: e.tensor_copy(
                  out=ridx[:, a::2], in_=w32a[:, :]), inc=s_v)
            em.op("vector", lambda e, so=so: e.tensor_scalar(out=w32a[:, :], in0=uf[:, :],
                  scalar1=float(RPR - SPC), scalar2=None, op0=ALU.mult), inc=s_v)
            em.wait("vector", s_v)
            em.op("vector", lambda e, so=so: e.tensor_tensor(out=w32a[:, :], in0=w32a[:, :],
                  in1=so[:, :], op=ALU.add), inc=s_v)
            em.wait("vector", s_v)
            em.op("vector", lambda e, er=er: e.tensor_tensor(out=w32a[:, :], in0=w32a[:, :],
                  in1=er[:, :], op=ALU.add), inc=s_v)
            em.wait("vector", s_v)
            em.op("vector", lambda e, ka=ka: e.tensor_tensor(out=w32a[:, :], in0=w32a[:, :],
                  in1=ka[:, :], op=ALU.mult), inc=s_v)
            em.op("vector", lambda e, ka=ka: e.tensor_scalar(out=w32b[:, :], in0=ka[:, :],
                  scalar1=-1.0, scalar2=1.0, op0=ALU.mult, op1=ALU.add), inc=s_v)
            em.wait("vector", s_v)
            em.op("vector", lambda e: e.tensor_scalar(out=w32b[:, :], in0=w32b[:, :],
                  scalar1=trow[:, 0:1], scalar2=None, op0=ALU.mult), inc=s_v)
            em.wait("vector", s_v)
            em.op("vector", lambda e: e.tensor_tensor(out=w32a[:, :], in0=w32a[:, :],
                  in1=w32b[:, :], op=ALU.add), inc=s_v)
            em.wait("vector", s_v)
            em.op("vector", lambda e, a=a: e.tensor_copy(
                  out=idxsc[:, a::2], in_=w32a[:, :]), inc=s_v)
            em.op("vector", lambda e, a=a: e.tensor_copy(
                  out=TG[:, 2 * a::4], in_=tgrid[:, :]), inc=s_v)
            em.op("vector", lambda e, a=a, gk=gk: e.tensor_copy(
                  out=TG[:, 2 * a + 1::4], in_=gk[:, :]), inc=s_v)
        em.op("vector", lambda e: e.memset(zdt[:, :], 0.0), inc=s_v)
        scat_ready = s_v.n

        em.wait("gpsimd", s_v, scat_ready)
        dma("gpsimd", g_gp, dtab_h.rearrange("(p a) c -> p a c", p=128),
            zdt[:, :].rearrange("p (a c) -> p a c", c=2))
        em.waitm("gpsimd", g_gp)
        for t in range(NT):
            for a in range(2):
                col = 2 * t + a
                sgx = g_gp.nxt()
                em.wait("gpsimd", sgx)
                em.op("gpsimd", lambda e, col=col: e.indirect_dma_start(
                    out=dtab_h,
                    out_offset=bass.IndirectOffsetOnAxis(ap=idxsc[:, col:col + 1], axis=0),
                    in_=TG[:, 2 * col:2 * col + 2], in_offset=None), inc=sgx, k=16)
        em.waitm("gpsimd", g_gp)
        em.op("gpsimd", lambda e: e.collective_compute(
            "AllGather", ALU.bypass, replica_groups=[list(range(NCORES))],
            ins=[dtab_h], outs=[dtag_h]), inc=s_cc)
        em.waitm("vector", g_sync, startup_sync)
        em.op("vector", lambda e: e.tensor_copy(out=cgridu[:, :], in_=cgrid[:, :]), inc=s_v)
        cg_u32 = s_v.n
        em.wait("gpsimd", s_cc, 3)
        em.wait("gpsimd", s_v, cg_u32)
        dslk_sum = 0
        for v in range(NCORES):
            sgx = g_gp.nxt()
            em.wait("gpsimd", sgx)
            if v > 0:
                em.wait("gpsimd", s_v, dslk_sum)
            em.op("gpsimd", lambda e, v=v: e.indirect_dma_start(
                out=dslk_v[:, :], out_offset=None,
                in_=dtag_h.rearrange("(a b) c -> a (b c)", b=128),
                in_offset=bass.IndirectOffsetOnAxis(ap=cgridu[:, 0:1], axis=0),
                element_offset=v * DTR * 2), inc=sgx, k=16)
            vm = g_gp.mark()
            em.waitm("vector", g_gp, vm)
            if v == 0:
                em.op("vector", lambda e: e.tensor_copy(out=dslk_all[:, :],
                      in_=dslk_v[:, :]), inc=s_v)
            else:
                em.op("vector", lambda e: e.tensor_tensor(out=dslk_all[:, :],
                      in0=dslk_all[:, :], in1=dslk_v[:, :], op=ALU.add), inc=s_v)
            em.wait("vector", s_v)
            dslk_sum = s_v.n
        dslk_mark = g_gp.mark()
        es_p2.close()
        es_r.close()

        em.waitm("tensor", g_gp, wqkv_gp)
        em.waitm("vector", g_gp, startup_gp)
        em.waitm("vector", g_sync, startup_sync)
        em.op("vector", lambda e: e.memset(
            Vs[:, :].rearrange("p (h z) -> p h z", h=H)[:, :, 128:129], 1.0), inc=s_v)

        attn_store = [None, None]
        for b in range(BC):
            bt0 = b * L
            em.waitm("scalar", g_sq)
            em.wait("scalar", s_t)
            dma_fn("scalar", g_sq, lambda e, bt0=bt0: e.dma_start(
                out=xnTb[:, :].rearrange("p (j t) -> p j t", j=JC),
                in_=xnT_h[:, :, bt0:bt0 + L]))
            xnTb_ld = g_sq.mark()

            pq = [PA[:, 0:512], PA[:, 512:1024]]
            pq_free = [0, 0]
            em.wait("tensor", s_v)
            em.waitm("tensor", g_sq, xnTb_ld)
            for tt in range(L // 512):
                for dq in range(JC):
                    k = tt * JC + dq
                    pqi = pq[k % 2]
                    em.wait("tensor", s_v, pq_free[k % 2])
                    for j in range(JC):
                        em.op("tensor", lambda e, j=j, dq=dq, tt=tt, pqi=pqi: e.matmul(
                            out=pqi[:, :],
                            lhsT=Wq_sb[:, j * C + dq * 128: j * C + (dq + 1) * 128],
                            rhs=xnTb[:, j * L + tt * 512: j * L + (tt + 1) * 512],
                            start=(j == 0), stop=(j == JC - 1)),
                            inc=(s_t if j == JC - 1 else None))
                    qmm = s_t.n
                    em.wait("vector", s_t, qmm)
                    em.op("vector", lambda e, dq=dq, pqi=pqi: e.tensor_scalar(
                        out=tq1[:, :], in0=pqi[:, :], scalar1=bqT[:, dq:dq + 1],
                        scalar2=0.0, op0=ALU.add, op1=ALU.min), inc=s_v)
                    em.wait("scalar", s_v)
                    em.op("scalar", lambda e: e.activation(out=tq2[:, :], in_=tq1[:, :],
                          func=AF.Exp), inc=s_s)
                    em.wait("vector", s_s)
                    em.op("vector", lambda e, dq=dq, pqi=pqi: e.tensor_scalar(
                        out=tq1[:, :], in0=pqi[:, :], scalar1=bqT[:, dq:dq + 1],
                        scalar2=0.0, op0=ALU.add, op1=ALU.max), inc=s_v)
                    em.wait("vector", s_v)
                    em.op("vector", lambda e, dq=dq, tt=tt: e.tensor_tensor(
                        out=QT[:, dq * L + tt * 512: dq * L + (tt + 1) * 512],
                        in0=tq1[:, :], in1=tq2[:, :], op=ALU.add), inc=s_v)
                    pq_free[k % 2] = s_v.n
            qt_done = s_v.n

            pk = PA[:, 0:1024]
            pv = PA[:, 1024:2048]
            pkv = PB[:, 0:2048]
            em.wait("tensor", s_v)
            em.op("vector", lambda e: e.memset(KVacc[:, :], 0.0), inc=s_v)
            kv_added = s_v.n
            for t in range(L // 128):
                for (W_sb, bb, pdst) in ((Wk_sb, bkb, pk), (Wv_sb, bvb, pv)):
                    for half in range(2):
                        for j in range(JC):
                            em.op("tensor", lambda e, j=j, half=half, t=t, W_sb=W_sb, pdst=pdst: e.matmul(
                                out=pdst[:, half * 512:(half + 1) * 512],
                                lhsT=xnTb[:, j * L + t * 128: j * L + (t + 1) * 128],
                                rhs=W_sb[:, j * C + half * 512: j * C + (half + 1) * 512],
                                start=(j == 0), stop=False))
                        em.op("tensor", lambda e, half=half, bb=bb, pdst=pdst: e.matmul(
                            out=pdst[:, half * 512:(half + 1) * 512],
                            lhsT=onesrowb[:, :],
                            rhs=bb[:, half * 512:(half + 1) * 512],
                            start=False, stop=True),
                            inc=s_t)
                kv_mm = s_t.n
                em.wait("vector", s_t, kv_mm)
                em.op("vector", lambda e: e.tensor_scalar(out=tmka[:, :], in0=pk[:, :],
                      scalar1=0.0, scalar2=None, op0=ALU.min), inc=s_v)
                em.wait("scalar", s_v)
                em.op("scalar", lambda e: e.activation(out=tmkb[:, :], in_=tmka[:, :],
                      func=AF.Exp), inc=s_s)
                em.wait("vector", s_s)
                em.op("vector", lambda e: e.scalar_tensor_tensor(out=Kb[:, :], in0=pk[:, :],
                      scalar=0.0, in1=tmkb[:, :], op0=ALU.max, op1=ALU.add), inc=s_v)
                em.op("vector", lambda e: e.tensor_scalar(
                      out=Vs[:, :].rearrange("p (h z) -> p h z", h=H)[:, :, 0:128],
                      in0=pv[:, :].rearrange("p (h d) -> p h d", h=H),
                      scalar1=1.0 / L, scalar2=None, op0=ALU.mult), inc=s_v)
                kvin_done = s_v.n
                em.wait("tensor", s_v, kvin_done)
                em.wait("tensor", s_v, kv_added)
                for h in range(H):
                    em.op("tensor", lambda e, h=h, t=t: e.matmul(
                        out=pkv[:, h * 256: h * 256 + 129],
                        lhsT=Kb[:, h * 128:(h + 1) * 128],
                        rhs=Vs[:, h * 132: h * 132 + 129],
                        start=True, stop=True),
                        inc=(s_t if h == H - 1 else None))
                kv_mm2 = s_t.n
                em.wait("vector", s_t, kv_mm2)
                em.op("vector", lambda e: e.tensor_tensor(
                    out=KVacc[:, :].rearrange("p (h z) -> p h z", h=H)[:, :, 0:129],
                    in0=KVacc[:, :].rearrange("p (h z) -> p h z", h=H)[:, :, 0:129],
                    in1=pkv[:, :].rearrange("p (h z) -> p h z", h=H)[:, :, 0:129],
                    op=ALU.add), inc=s_v)
                kv_added = s_v.n
            em.wait("vector", s_v, kv_added)
            em.op("vector", lambda e: e.tensor_copy(
                out=KVs[:, :].rearrange("p (h z) -> p h z", h=H)[:, :, 0:129],
                in_=KVacc[:, :].rearrange("p (h z) -> p h z", h=H)[:, :, 0:129]), inc=s_v)
            kvs_done = s_v.n

            pao = [PA[:, 0:2048], PB[:, 0:2048]]
            pao_free = [0, 0]
            em.wait("tensor", s_v, kvs_done)
            for t in range(L // 128):
                bi = t % 2
                pa = pao[bi]
                em.wait("tensor", s_v, pao_free[bi])
                for h in range(H):
                    em.op("tensor", lambda e, h=h, t=t, pa=pa: e.matmul(
                        out=pa[:, h * 256: h * 256 + 129],
                        lhsT=QT[:, h * L + t * 128: h * L + (t + 1) * 128],
                        rhs=KVs[:, h * 132: h * 132 + 129],
                        start=True, stop=True),
                        inc=(s_t if h == H - 1 else None))
                ao_mm = s_t.n
                em.wait("vector", s_t, ao_mm)
                em.op("vector", lambda e, pa=pa: e.tensor_scalar(
                      out=zden[:, :].rearrange("p (h o) -> p h o", o=1),
                      in0=pa[:, :].rearrange("p (h z) -> p h z", h=H)[:, :, 128:129],
                      scalar1=EPS_ATT, scalar2=None, op0=ALU.add), inc=s_v)
                em.wait("vector", s_v)
                em.op("vector", lambda e: e.reciprocal(out=zrec[:, :], in_=zden[:, :]), inc=s_v)
                em.wait("vector", s_v)
                ab = attnbuf[bi]
                em.waitm("vector", g_sync, attn_store[bi])
                for h in range(H):
                    em.op("vector", lambda e, h=h, pa=pa, ab=ab: e.tensor_scalar(
                          out=ab[:, h * 128:(h + 1) * 128],
                          in0=pa[:, h * 256: h * 256 + 128],
                          scalar1=zrec[:, h:h + 1], scalar2=float(L),
                          op0=ALU.mult, op1=ALU.mult),
                          inc=(s_v if h == H - 1 else None))
                pao_free[bi] = s_v.n
                em.wait("sync", s_v)
                dma_fn("sync", g_sync, lambda e, t=t, bt0=bt0, ab=ab: e.dma_start(
                    out=attn_h[bt0 + t * 128: bt0 + (t + 1) * 128, :], in_=ab[:, :]))
                attn_store[bi] = g_sync.mark()
        attn_all_stored = g_sync.mark()
        p3_t = s_t.n
        es_att.close()
        es_p4 = contextlib.ExitStack()
        w1_sb = es_p4.enter_context(nc.sbuf_tensor("w1_sb", [128, JC * FF], BF16))
        w2_sb = es_p4.enter_context(nc.sbuf_tensor("w2_sb", [128, FCN * C], BF16))
        h_sb = es_p4.enter_context(nc.sbuf_tensor("h_sb", [128, FCN * 512], BF16))
        xg1 = [es_p4.enter_context(nc.sbuf_tensor("xg10", [128, C], F32))]
        xgT = es_p4.enter_context(nc.sbuf_tensor("xgT0", [128, JC * 512], BF16))
        yw = [es_p4.enter_context(nc.sbuf_tensor("yw0", [128, C], F32)),
              es_p4.enter_context(nc.sbuf_tensor("yw1", [128, C], F32))]
        b2bc = es_p4.enter_context(nc.sbuf_tensor("b2bc", [128, C], BF16))
        idxf4 = es_p4.enter_context(nc.sbuf_tensor("idxf4", [128, E * (SPC // 128)], F32))
        idxu4 = es_p4.enter_context(nc.sbuf_tensor("idxu4", [128, E * (SPC // 128)], U32))
        gts4 = es_p4.enter_context(nc.sbuf_tensor("gts4", [128, E * (SPC // 128)], F32))
        ph = [PA[:, 0:512], PA[:, 512:1024]]
        pst = [PA[:, 1024:1536], PA[:, 1536:2048]]
        py = [PB[:, 0:1024], PB[:, 1024:2048]]
        NKT = SPC // 128

        def load_w1(e_idx):
            src = w1_h[e_idx].rearrange("(j p) f -> p j f", p=128)
            dst = w1_sb[:, :].rearrange("p (j f) -> p j f", j=JC)
            dma("gpsimd", g_gp, dst, src)

        def load_w2(e_idx):
            src = w2_h[e_idx].rearrange("(f p) d -> p f d", p=128)
            dst = w2_sb[:, :].rearrange("p (f d) -> p f d", f=FCN)
            dma("gpsimd", g_gp, dst, src)
            dma("gpsimd", g_gp, b2bc[:, :],
                b2row_h[0:1, e_idx * C:(e_idx + 1) * C].broadcast_to([128, C]))

        em.waitm("vector", g_sync)
        em.waitm("vector", g_sq)
        em.op("vector", lambda e: e.memset(h_sb[:, 0:4096], 0.0), inc=s_v)
        em.wait("gpsimd", s_v)
        dma("gpsimd", g_gp, yslots_h[E * SPC:YR, :].rearrange("(q p) c -> p q c", p=128),
            h_sb[:, 0:4096].rearrange("p (q c) -> p q c", q=(YR - E * SPC) // 128))
        yw_trash_mark = g_gp.mark()
        em.waitm("scalar", g_gp, yw_trash_mark)

        em.wait("gpsimd", s_t, p3_t)
        em.wait("gpsimd", s_v)
        em.waitm("gpsimd", g_sync)
        em.waitm("gpsimd", g_sq)
        load_w1(0)
        load_w2(0)
        w12_ld = g_gp.mark()
        em.waitm("scalar", g_sync, startup_sync)
        em.op("scalar", lambda e: e.activation(out=wtmp[:, 3:4], in_=epsln[:, :],
              func=AF.Gelu_apprx_tanh), inc=s_s)

        em.wait("tensor", s_v)
        em.op("tensor", lambda e: e.transpose(
            out=pst[0][:, 0:E * NKT], in_=dslk_all[0:E * NKT, 0:256:2],
            identity=ident[0:E * NKT, 0:E * NKT]))
        em.op("tensor", lambda e: e.transpose(
            out=pst[1][:, 0:E * NKT], in_=dslk_all[0:E * NKT, 1:256:2],
            identity=ident[0:E * NKT, 0:E * NKT]), inc=s_t)
        dtr_done = s_t.n
        em.wait("vector", s_t, dtr_done)
        em.op("vector", lambda e: e.tensor_copy(out=idxf4[:, :], in_=pst[0][:, 0:E * NKT]), inc=s_v)
        em.wait("vector", s_v)
        em.op("vector", lambda e: e.tensor_copy(out=gts4[:, :], in_=pst[1][:, 0:E * NKT]), inc=s_v)
        em.wait("vector", s_v)
        em.op("vector", lambda e: e.tensor_copy(out=idxu4[:, :], in_=idxf4[:, :]), inc=s_v)
        pst_free = [s_v.n, s_v.n]
        idx_u32 = s_v.n

        xg_ld = [None, None]
        xg_free = [0, 0]
        xgT_free = [0]
        ph_free = [0, 0]
        py_free = [0, 0]
        yw_free = [None, None]
        h_consumed = 0
        idx_ready = 0
        warm_done = False

        for e_idx in range(E):

            for kt in range(NKT // 4 + (1 if NKT % 4 else 0)):
                g0 = kt * 4
                ng = min(4, NKT - g0)
                NS = ng * 128
                bi = kt % 2
                em.wait("gpsimd", s_v, idx_u32)
                em.wait("gpsimd", s_cc, 2)
                em.wait("tensor", s_t, xgT_free[0])
                for g in range(ng):
                    gb = 0
                    em.wait("gpsimd", s_t, xg_free[gb])
                    sgx = g_gp.nxt()
                    em.wait("gpsimd", sgx)
                    em.op("gpsimd", lambda e, g=g, g0=g0, gb=gb, e_idx=e_idx: e.indirect_dma_start(
                        out=xg1[gb][:, :], out_offset=None,
                        in_=xall_h,
                        in_offset=bass.IndirectOffsetOnAxis(
                            ap=idxu4[:, e_idx * NKT + g0 + g:e_idx * NKT + g0 + g + 1],
                            axis=0)), inc=sgx, k=16)
                    xg_ld[gb] = g_gp.mark()
                    em.waitm("tensor", g_gp, xg_ld[gb])
                    for jh in range(2):
                        pb = jh
                        em.wait("tensor", s_v, pst_free[pb])
                        for jj in range(4):
                            j = jh * 4 + jj
                            em.op("tensor", lambda e, j=j, jj=jj, gb=gb, pb=pb: e.transpose(
                                out=pst[pb][:, jj * 128:(jj + 1) * 128],
                                in_=xg1[gb][:, j * 128:(j + 1) * 128],
                                identity=ident[:, :]),
                                inc=(s_t if jj == 3 else None))
                        trj = s_t.n
                        em.wait("vector", s_t, trj)
                        em.op("vector", lambda e, jh=jh, g=g, pb=pb: e.tensor_copy(
                            out=xgT[:, :].rearrange("p (j s) -> p j s", j=JC)
                                [:, jh * 4:(jh + 1) * 4, g * 128:(g + 1) * 128],
                            in_=pst[pb][:, 0:512].rearrange("p (j s) -> p j s", j=4)),
                            inc=s_v)
                        pst_free[pb] = s_v.n
                    xg_free[gb] = s_t.n
                xgT_ready = s_v.n

                if e_idx == 0 and kt == 0 and not warm_done:
                    warm_done = True
                    em.waitm("tensor", g_gp, w12_ld)
                    em.wait("tensor", s_v, xgT_ready)
                    for j in range(JC):
                        em.op("tensor", lambda e, j=j: e.matmul(
                            out=ph[0][:, 0:NS],
                            lhsT=w1_sb[:, j * FF: j * FF + 128],
                            rhs=xgT[:, j * 512: j * 512 + NS],
                            start=(j == 0), stop=(j == JC - 1)),
                            inc=(s_t if j == JC - 1 else None))

                em.wait("tensor", s_t, h_consumed)
                em.waitm("tensor", g_gp, w12_ld)
                em.wait("tensor", s_v, xgT_ready)
                for fc in range(FCN):
                    phi = ph[fc % 2]
                    em.wait("tensor", s_s, ph_free[fc % 2])
                    for j in range(JC):
                        em.op("tensor", lambda e, j=j, fc=fc, phi=phi, NS=NS: e.matmul(
                            out=phi[:, 0:NS],
                            lhsT=w1_sb[:, j * FF + fc * 128: j * FF + (fc + 1) * 128],
                            rhs=xgT[:, j * 512: j * 512 + NS],
                            start=(j == 0), stop=(j == JC - 1)),
                            inc=(s_t if j == JC - 1 else None))
                    hmm = s_t.n
                    em.wait("scalar", s_t, hmm)
                    em.op("scalar", lambda e, fc=fc, e_idx=e_idx, phi=phi, NS=NS: e.activation(
                        out=h_sb[:, fc * 512: fc * 512 + NS], in_=phi[:, 0:NS],
                        func=AF.Gelu_apprx_tanh,
                        bias=b1T[:, e_idx * FCN + fc: e_idx * FCN + fc + 1],
                        scale=1.0), inc=s_s)
                    ph_free[fc % 2] = s_s.n
                xgT_free[0] = s_t.n
                h_done = s_s.n

                em.wait("tensor", s_s, h_done)
                for g in range(ng):
                    yb = g % 2
                    em.wait("tensor", s_v, py_free[yb])
                    for dh in range(2):
                        for fc in range(FCN):
                            em.op("tensor", lambda e, g=g, dh=dh, fc=fc, yb=yb: e.matmul(
                                out=py[yb][:, dh * 512:(dh + 1) * 512],
                                lhsT=h_sb[:, fc * 512 + g * 128: fc * 512 + (g + 1) * 128],
                                rhs=w2_sb[:, fc * C + dh * 512: fc * C + (dh + 1) * 512],
                                start=(fc == 0), stop=(fc == FCN - 1)),
                                inc=(s_t if (fc == FCN - 1 and dh == 1) else None))
                    ymm = s_t.n
                    em.wait("vector", s_t, ymm)
                    em.waitm("vector", g_gp, w12_ld)
                    em.waitm("vector", g_gp, yw_free[yb])
                    em.op("vector", lambda e, yb=yb: e.tensor_tensor(
                        out=py[yb][:, :], in0=py[yb][:, :], in1=b2bc[:, :],
                        op=ALU.add), inc=s_v)
                    em.wait("vector", s_v)
                    em.op("vector", lambda e, g=g, g0=g0, yb=yb, e_idx=e_idx: e.tensor_scalar(
                        out=yw[yb][:, :], in0=py[yb][:, :],
                        scalar1=gts4[:, e_idx * NKT + g0 + g:e_idx * NKT + g0 + g + 1],
                        scalar2=None, op0=ALU.mult), inc=s_v)
                    py_free[yb] = s_v.n
                    em.wait("gpsimd", s_v)
                    row0 = e_idx * SPC + (g0 + g) * 128
                    dma("gpsimd", g_gp, yslots_h[row0:row0 + 128, :], yw[yb][:, :])
                    yw_free[yb] = g_gp.mark()
                h_consumed = s_t.n
            idx_ready = s_v.n
            if e_idx + 1 < E:
                em.wait("gpsimd", s_t)
                em.wait("gpsimd", s_v)
                load_w1(e_idx + 1)
                load_w2(e_idx + 1)
                w12_ld = g_gp.mark()
        ffn_all = g_gp.mark()
        p4_t = s_t.n
        es_p4.close()

        em.waitm("gpsimd", g_gp, ffn_all)
        em.op("gpsimd", lambda e: e.collective_compute(
            "AllGather", ALU.bypass, replica_groups=[list(range(NCORES))],
            ins=[yslots_h], outs=[ytall_h]), inc=s_cc)
        es_p5 = contextlib.ExitStack()
        att_t = [es_p5.enter_context(nc.sbuf_tensor("att0", [128, C], F32)),
                 es_p5.enter_context(nc.sbuf_tensor("att1", [128, C], F32))]
        c1b = [es_p5.enter_context(nc.sbuf_tensor("c1b0", [128, C], F32)),
               es_p5.enter_context(nc.sbuf_tensor("c1b1", [128, C], F32))]
        c2b = [es_p5.enter_context(nc.sbuf_tensor("c2b0", [128, C], F32)),
               es_p5.enter_context(nc.sbuf_tensor("c2b1", [128, C], F32))]
        osb = [es_p5.enter_context(nc.sbuf_tensor("osb0", [128, C], F32)),
               es_p5.enter_context(nc.sbuf_tensor("osb1", [128, C], F32))]

        em.waitm("sync", g_sync)
        em.waitm("sync", g_gp)
        em.wait("sync", s_t, p4_t)
        em.wait("sync", s_v)
        em.wait("gpsimd", s_cc, 4)
        ld_done = [None, None]
        gat_done = [None, None]
        comb_done = [0, 0]
        osb_free = [None, None]
        for t in range(NT):
            bi = t % 2
            em.waitm("sync", g_sync, ld_done[bi])
            em.wait("sync", s_v, comb_done[bi])
            dma("sync", g_sync, att_t[bi][:, :], attn_h[t * 128:(t + 1) * 128, :])
            dma("sync", g_sync, xbuf[bi][:, :], x_h[t * 128:(t + 1) * 128, :])
            ld_done[bi] = g_sync.mark()
            em.wait("gpsimd", s_v, comb_done[bi])
            for a in range(2):
                sgx = g_gp.nxt()
                em.wait("gpsimd", sgx)
                em.op("gpsimd", lambda e, t=t, a=a, bi=bi: e.indirect_dma_start(
                    out=(c1b if a == 0 else c2b)[bi][:, :], out_offset=None,
                    in_=ytall_h,
                    in_offset=bass.IndirectOffsetOnAxis(
                        ap=ridx[:, 2 * t + a:2 * t + a + 1], axis=0)), inc=sgx, k=16)
            gat_done[bi] = g_gp.mark()
            em.waitm("vector", g_sync, ld_done[bi])
            em.waitm("vector", g_gp, gat_done[bi])
            em.waitm("vector", g_sync, osb_free[bi])
            em.op("vector", lambda e, bi=bi: e.tensor_tensor(out=osb[bi][:, :],
                  in0=xbuf[bi][:, :], in1=att_t[bi][:, :], op=ALU.add), inc=s_v)
            em.wait("vector", s_v)
            em.op("vector", lambda e, bi=bi: e.tensor_tensor(out=osb[bi][:, :],
                  in0=osb[bi][:, :], in1=c1b[bi][:, :], op=ALU.add), inc=s_v)
            em.wait("vector", s_v)
            em.op("vector", lambda e, bi=bi: e.tensor_tensor(out=osb[bi][:, :],
                  in0=osb[bi][:, :], in1=c2b[bi][:, :], op=ALU.add), inc=s_v)
            comb_done[bi] = s_v.n
            em.wait("sync", s_v, comb_done[bi])
            dma_fn("sync", g_sync, lambda e, t=t, bi=bi: e.dma_start(
                out=y_h[t * 128:(t + 1) * 128, :], in_=osb[bi][:, :]))
            osb_free[bi] = g_sync.mark()
        es_p5.close()


        if DEBUG:
            em.wait("sync", s_cc, 4)
            em.wait("sync", s_v)
            dma("sync", g_sync, dt_dbg_h, dslk_all[:, :])
            dma("sync", g_sync, ys_dbg_h, yslots_h)

        em.waitm("gpsimd", g_sync)
        em.waitm("gpsimd", g_sq)
        em.waitm("gpsimd", g_gp)

        block = es.enter_context(nc.Block())
        em.replay(block)
    return nc


def _per_core_inputs(inp):
    x = np.asarray(inp["L0_attn_x"], dtype=np.float32).reshape(16, L, C)
    wq = np.ascontiguousarray(np.asarray(inp["L0_attn_Wq"], dtype=np.float32))
    wk = np.ascontiguousarray(np.asarray(inp["L0_attn_Wk"], dtype=np.float32))
    wv = np.ascontiguousarray(np.asarray(inp["L0_attn_Wv"], dtype=np.float32))
    bq = np.asarray(inp["L0_attn_bq"], dtype=np.float32)
    bk = np.asarray(inp["L0_attn_bk"], dtype=np.float32)
    bv = np.asarray(inp["L0_attn_bv"], dtype=np.float32)
    wr = np.asarray(inp["L0_ffn_wr"], dtype=np.float32)
    w1 = np.ascontiguousarray(np.asarray(inp["L0_ffn_w1"], dtype=np.float32))
    b1 = np.asarray(inp["L0_ffn_b1"], dtype=np.float32)
    w2 = np.ascontiguousarray(np.asarray(inp["L0_ffn_w2"], dtype=np.float32))
    b2 = np.asarray(inp["L0_ffn_b2"], dtype=np.float32)

    bqT = np.ascontiguousarray(bq.reshape(JC, 128).T)
    wrT = np.ascontiguousarray(wr.T.reshape(JC, 128, E).transpose(1, 0, 2).reshape(128, JC * E))
    b1T = np.ascontiguousarray(b1.reshape(E, FCN, 128).transpose(2, 0, 1).reshape(128, E * FCN))
    b2row = np.ascontiguousarray(b2.reshape(1, E * C))
    ident = np.eye(128, dtype=np.float32)
    triu = np.triu(np.ones((128, 128), dtype=np.float32), 1)
    onesrow = np.ones((1, 128), dtype=np.float32)
    onescol = np.ones((128, 1), dtype=np.float32)
    iota8b = np.tile(np.arange(E, dtype=np.float32), (128, 1))
    pcol = np.arange(128, dtype=np.float32).reshape(128, 1)
    prange = np.arange(128)
    trow = ((prange % NCORES) * RPR + E * SPC + prange // NCORES).astype(np.float32).reshape(128, 1)
    NKT = SPC // 128
    cg = np.zeros(128, np.float32)
    for p in range(E * NKT):
        cg[p] = (p // NKT) * (SPC // 128) * 0 + p
    for p in range(128):
        if p < E * NKT:
            e_, k_ = p // NKT, p % NKT
            cg[p] = e_ * (SPC // 128) + k_
        else:
            cg[p] = 0
    cgrid = cg.reshape(128, 1)

    shared = {"wq": wq, "wk": wk, "wv": wv, "bqT_d": bqT, "bk": bk.reshape(1, C),
              "bv": bv.reshape(1, C), "wrT_d": wrT, "b1T_d": b1T, "b2row_d": b2row,
              "w1": w1, "w2": w2, "ident_d": ident, "triu_d": triu,
              "onesrow_d": onesrow, "onesrowb_d": onesrow, "onescol_d": onescol,
              "iota8b_d": iota8b, "pcol_d": pcol, "trow_d": trow}
    maps = []
    for i in range(NCORES):
        m = dict(shared)
        m["x"] = np.ascontiguousarray(x[BC * i: BC * (i + 1)].reshape(TK, C))
        m["cmask_d"] = (np.arange(NCORES) < i).astype(np.float32).reshape(NCORES, 1)
        m["cgrid_d"] = np.ascontiguousarray(cgrid + i * (RPR // 128))
        m["tgrid_d"] = np.ascontiguousarray(
            (np.arange(NT, dtype=np.float32)[None, :] * 128
             + np.arange(128, dtype=np.float32)[:, None] + TK * i))
        maps.append(m)
    return maps


def kernel(**inputs):
    nc = build()
    in_maps = _per_core_inputs(inputs)
    res = run_bass_kernel_spmd(nc, in_maps, core_ids=list(range(NCORES)))
    out = np.concatenate([np.asarray(r["y"]) for r in res.results], axis=0)
    return out.reshape(16, L, C)


# version 07 (final)
# speedup vs baseline: 1.2034x; 1.2034x over previous
import contextlib
import numpy as np

import concourse.bass as bass
import concourse.mybir as mybir
from concourse.bass_utils import run_bass_kernel_spmd

F32 = mybir.dt.float32
BF16 = mybir.dt.bfloat16
U32 = mybir.dt.uint32
AF = mybir.ActivationFunctionType
ALU = mybir.AluOpType
AX = mybir.AxisListType

DEBUG = False

NCORES = 8
BC = 2
L = 2048
C = 1024
H = 8
DH = 128
TK = BC * L
NT = TK // 128
JC = C // 128
E = 8
FF = 4096
FCN = FF // 128
CAP = 10240
SPC = CAP // NCORES
RPR = 16384
DTR = NCORES * RPR
YR = 10752
EPS_ATT = 1e-6
EPS_LN = 1e-5

ENGS = ("sync", "scalar", "vector", "tensor", "gpsimd")


class S:
    def __init__(self, h):
        self.h = h
        self.n = 0


class DS:
    def __init__(self, sems):
        self.sems = sems
        self.i = 0

    def nxt(self):
        s = self.sems[self.i % len(self.sems)]
        self.i += 1
        return s

    def mark(self):
        return [s.n for s in self.sems]


class EM:
    def __init__(self):
        self.q = {k: [] for k in ENGS}
        self.auto = {}

    def op(self, eng, fn, inc=None, k=1):
        s_auto = self.auto.get(eng)
        if s_auto is not None and (inc is None or inc is s_auto):
            self.q[eng].append(("wait", s_auto, s_auto.n, 0))
            inc, k = s_auto, 1
        self.q[eng].append(("op", fn, inc, k))
        if inc is not None:
            inc.n += k

    def wait(self, eng, sem, val=None):
        v = sem.n if val is None else val
        self.q[eng].append(("wait", sem, v, 0))

    def waitm(self, eng, grp, mark=None):
        if mark is None:
            mark = grp.mark()
        for s, v in zip(grp.sems, mark):
            if v > 0:
                self.wait(eng, s, v)

    def replay(self, block):
        def make(name):
            items = self.q[name]
            def body(e):
                for it in items:
                    if it[0] == "wait":
                        e.wait_ge(it[1].h, it[2])
                    else:
                        ins = it[1](e)
                        if it[2] is not None:
                            ins.then_inc(it[2].h, it[3])
            return body
        for name in ENGS:
            getattr(block, name)(make(name))


def build():
    nc = bass.Bass("TRN2", target_bir_lowering=False)

    okind = "ExternalOutput" if DEBUG else "Internal"
    x_h = nc.dram_tensor("x", [TK, C], F32, kind="ExternalInput").ap()
    wq_h = nc.dram_tensor("wq", [C, C], F32, kind="ExternalInput").ap()
    wk_h = nc.dram_tensor("wk", [C, C], F32, kind="ExternalInput").ap()
    wv_h = nc.dram_tensor("wv", [C, C], F32, kind="ExternalInput").ap()
    bqT_h = nc.dram_tensor("bqT_d", [128, JC], F32, kind="ExternalInput").ap()
    bk_h = nc.dram_tensor("bk", [1, C], F32, kind="ExternalInput").ap()
    bv_h = nc.dram_tensor("bv", [1, C], F32, kind="ExternalInput").ap()
    wrT_h = nc.dram_tensor("wrT_d", [128, JC * E], F32, kind="ExternalInput").ap()
    b1T_h = nc.dram_tensor("b1T_d", [128, E * FCN], F32, kind="ExternalInput").ap()
    w1_h = nc.dram_tensor("w1", [E, C, FF], F32, kind="ExternalInput").ap()
    w2_h = nc.dram_tensor("w2", [E, FF, C], F32, kind="ExternalInput").ap()
    ident_h = nc.dram_tensor("ident_d", [128, 128], F32, kind="ExternalInput").ap()
    triu_h = nc.dram_tensor("triu_d", [128, 128], F32, kind="ExternalInput").ap()
    onesrow_h = nc.dram_tensor("onesrow_d", [1, 128], F32, kind="ExternalInput").ap()
    onesrowb_h = nc.dram_tensor("onesrowb_d", [1, 128], F32, kind="ExternalInput").ap()
    onescol_h = nc.dram_tensor("onescol_d", [128, 1], F32, kind="ExternalInput").ap()
    cmask_h = nc.dram_tensor("cmask_d", [E, 1], F32, kind="ExternalInput").ap()
    tgrid_h = nc.dram_tensor("tgrid_d", [128, NT], F32, kind="ExternalInput").ap()
    iota8b_h = nc.dram_tensor("iota8b_d", [128, E], F32, kind="ExternalInput").ap()
    pcol_h = nc.dram_tensor("pcol_d", [128, 1], F32, kind="ExternalInput").ap()
    trow_h = nc.dram_tensor("trow_d", [128, 1], F32, kind="ExternalInput").ap()
    cgrid_h = nc.dram_tensor("cgrid_d", [128, 1], F32, kind="ExternalInput").ap()
    b2row_h = nc.dram_tensor("b2row_d", [1, E * C], F32, kind="ExternalInput").ap()

    y_h = nc.dram_tensor("y", [TK, C], F32, kind="ExternalOutput").ap()
    xtok_h = nc.dram_tensor("xtok", [TK, C], F32).ap()
    xall_h = nc.dram_tensor("xall", [NCORES * TK, C], F32).ap()
    dtab_h = nc.dram_tensor("dtab", [DTR, 2], F32).ap()
    dtag_h = nc.dram_tensor("dtag", [NCORES * DTR, 2], F32).ap()
    yslots_h = nc.dram_tensor("yslots", [YR, C], BF16).ap()
    ytall_h = nc.dram_tensor("ytall", [NCORES * YR, C], BF16).ap()
    if DEBUG:
        xnT_h = nc.dram_tensor("xnT_s", [128, JC, TK], BF16, kind=okind).ap()
        attn_h = nc.dram_tensor("attn_s", [TK, C], F32, kind=okind).ap()
        dt_dbg_h = nc.dram_tensor("dt_dbg", [128, 256], F32, kind="ExternalOutput").ap()
        ys_dbg_h = nc.dram_tensor("ys_dbg", [YR, C], BF16, kind="ExternalOutput").ap()
    else:
        xnT_h = nc.dram_tensor("xnT_s", [128, JC, TK], BF16).ap()
        attn_h = nc.dram_tensor("attn_s", [TK, C], F32).ap()
    cci_h = nc.dram_tensor("cc_in", [1, E], F32).ap()
    cco_h = nc.dram_tensor("cc_out", [NCORES, E], F32).ap()
    if DEBUG:
        gates_dbg_h = nc.dram_tensor("gates_dbg", [128, NT * E], F32, kind="ExternalOutput").ap()
        pos_dbg_h = nc.dram_tensor("pos_dbg", [128, NT * E], F32, kind="ExternalOutput").ap()
        h_dbg_h = nc.dram_tensor("h_dbg", [128, FCN * 512], BF16, kind="ExternalOutput").ap()
        x4_dbg_h = nc.dram_tensor("x4_dbg", [128, JC * 512], BF16, kind="ExternalOutput").ap()
        w1_dbg_h = nc.dram_tensor("w1_dbg", [128, JC * 1024], BF16, kind="ExternalOutput").ap()
        m_dbg_h = nc.dram_tensor("m_dbg", [128, 512], F32, kind="ExternalOutput").ap()
        pre_dbg_h = nc.dram_tensor("pre_dbg", [128, 512], F32, kind="ExternalOutput").ap()

    es = contextlib.ExitStack()
    with es:
        def sb(name, shape, dt):
            return es.enter_context(nc.sbuf_tensor(name, shape, dt))

        def sem(name):
            return S(es.enter_context(nc.semaphore(name)))

        ident = sb("ident", [128, 128], F32)
        triu = sb("triu", [128, 128], F32)
        onesrow = sb("onesrow", [1, 128], F32)
        onesrowb = sb("onesrowb", [1, 128], BF16)
        onescol = sb("onescol", [128, 1], F32)
        cmask = sb("cmask", [E, 1], F32)
        bqT = sb("bqT", [128, JC], F32)
        bkb = sb("bkb", [1, C], BF16)
        bvb = sb("bvb", [1, C], BF16)
        wrT = sb("wrT", [128, JC * E], F32)
        b1T = sb("b1T", [128, E * FCN], F32)
        tgrid = sb("tgrid", [128, NT], F32)
        iota8b = sb("iota8b", [128, E], F32)
        pcol = sb("pcol", [128, 1], F32)
        trow = sb("trow", [128, 1], F32)
        cgridu = sb("cgridu", [128, 1], U32)
        cgrid = sb("cgrid", [128, 1], F32)
        dslk_all = sb("dslk_all", [128, 256], F32)
        dslk_v = sb("dslk_v", [128, 256], F32)
        offrow = sb("offrow", [1, E], F32)
        capbO = sb("capbO", [128, E], F32)
        ebc1280 = sb("ebc1280", [128, E], F32)
        ridx = sb("ridx", [128, NT * 2], U32)
        xtokb = [sb("xtokb0", [128, C], BF16), sb("xtokb1", [128, C], BF16)]

        running = sb("running", [1, E], F32)
        countsAll = sb("countsAll", [NCORES, E], F32)
        capm = sb("capm", [1, E], F32)
        capb = sb("capb", [128, E], F32)

        epsln = sb("epsln", [128, 1], F32)
        wtmp = sb("wtmp", [128, 4], F32)
        xbuf = [sb("xbuf0", [128, C], F32), sb("xbuf1", [128, C], F32)]
        PA = es.enter_context(nc.psum_tensor("PA", [128, 2048], F32))
        PB = es.enter_context(nc.psum_tensor("PB", [128, 2048], F32))

        em = EM()
        g_sync = DS([sem(f"g_sync{i}") for i in range(4)])
        g_sq = DS([sem(f"g_sq{i}") for i in range(4)])
        g_gp = DS([sem(f"g_gp{i}") for i in range(6)])
        s_cc = sem("s_cc")
        s_v = sem("s_v")
        s_s = sem("s_s")
        s_t = sem("s_t")
        em.auto = {"vector": s_v, "scalar": s_s}


        def dma(eng, grp, dst, src, accum=False):
            if eng in em.auto:
                em.wait(eng, em.auto[eng])
            s = grp.nxt()
            em.wait(eng, s)
            if accum:
                em.op(eng, lambda e, a=dst, b=src: e.dma_start(out=a, in_=b, accum_op=ALU.add),
                      inc=s, k=16)
            else:
                em.op(eng, lambda e, a=dst, b=src: e.dma_start(out=a, in_=b), inc=s, k=16)

        def dma_fn(eng, grp, fn):
            if eng in em.auto:
                em.wait(eng, em.auto[eng])
            s = grp.nxt()
            em.wait(eng, s)
            em.op(eng, fn, inc=s, k=16)

        dma("sync", g_sync, ident[:, :], ident_h)
        dma("sync", g_sync, triu[:, :], triu_h)
        dma("sync", g_sync, onesrow[:, :], onesrow_h)
        dma("sync", g_sync, onescol[:, :], onescol_h)
        dma("sync", g_sync, tgrid[:, :], tgrid_h)
        dma("sync", g_sync, iota8b[:, :], iota8b_h)
        dma("sync", g_sync, pcol[:, :], pcol_h)
        dma("sync", g_sync, trow[:, :], trow_h)
        dma("sync", g_sync, cgrid[:, :], cgrid_h)
        dma("sync", g_sync, cmask[:, :], cmask_h)
        dma("sync", g_sync, bqT[:, :], bqT_h)
        dma("sync", g_sync, wrT[:, :], wrT_h)
        dma("sync", g_sync, b1T[:, :], b1T_h)
        dma("gpsimd", g_gp, onesrowb[:, :], onesrowb_h)
        dma("gpsimd", g_gp, bkb[:, :], bk_h)
        dma("gpsimd", g_gp, bvb[:, :], bv_h)
        startup_sync = g_sync.mark()
        startup_gp = g_gp.mark()

        es_att = contextlib.ExitStack()
        def sba(name, shape, dt):
            return es_att.enter_context(nc.sbuf_tensor(name, shape, dt))
        Wq_sb = sba("Wq_sb", [128, JC * C], BF16)
        Wk_sb = sba("Wk_sb", [128, JC * C], BF16)
        Wv_sb = sba("Wv_sb", [128, JC * C], BF16)
        xnTb = sba("xnTb", [128, JC * L], BF16)
        QT = sba("QT", [128, H * L], BF16)
        Kb = sba("Kb", [128, C], BF16)
        Vs = sba("Vs", [128, H * 132], BF16)
        KVs = sba("KVs", [128, H * 132], BF16)
        KVacc = sba("KVacc", [128, H * 132], F32)
        tmka = sba("tmka", [128, C], F32)
        tmkb = sba("tmkb", [128, C], F32)
        tq1 = sba("tq1", [128, 512], F32)
        attnbuf = [sba("attnb0", [128, C], F32), sba("attnb1", [128, C], F32)]
        tq2 = sba("tq2", [128, 512], F32)
        zden = sba("zden", [128, H], F32)
        zrec = sba("zrec", [128, H], F32)

        es_r = contextlib.ExitStack()
        pos_all = es_r.enter_context(nc.sbuf_tensor("pos_all", [128, NT * E], F32))
        gates_all = es_r.enter_context(nc.sbuf_tensor("gates_all", [128, NT * E], F32))
        kept = es_r.enter_context(nc.sbuf_tensor("kept", [128, NT * E], F32))
        eq1_all = es_r.enter_context(nc.sbuf_tensor("eq1_all", [128, NT * E], F32))
        eq2p_all = es_r.enter_context(nc.sbuf_tensor("eq2p_all", [128, NT * E], F32))

        for W_sb, W_h in ((Wq_sb, wq_h), (Wk_sb, wk_h), (Wv_sb, wv_h)):
            src = W_h.rearrange("(j p) d -> p j d", p=128)
            dst = W_sb[:, :].rearrange("p (j d) -> p j d", j=JC)
            dma("gpsimd", g_gp, dst, src)
        wqkv_gp = g_gp.mark()

        es_p1 = contextlib.ExitStack()
        ptr = [PA[:, 0:1024], PA[:, 1024:2048]]
        plg = PB[:, 0:8]
        ppos = PB[:, 512:520]
        ptt = PB[0:1, 1024:1032]
        def sb1(name, shape, dt):
            return es_p1.enter_context(nc.sbuf_tensor(name, shape, dt))
        bstats = sb1("bstats", [128, 12], F32)
        gvar = sb1("gvar", [128, 4], F32)
        gmsq = sb1("gmsq", [128, 4], F32)
        ex2g = sb1("ex2g", [128, 4], F32)
        msum = sb1("msum", [128, 1], F32)
        exs = sb1("exs", [128, 1], F32)
        meanv = sb1("meanv", [128, 1], F32)
        msq2 = sb1("msq2", [128, 1], F32)
        varv = sb1("varv", [128, 1], F32)
        xn = sb1("xn", [128, C], F32)
        xnT32 = sb1("xnT32", [128, C], F32)
        xnTbf = sb1("xnTbf", [128, C], BF16)
        lgs = sb1("lgs", [128, E], F32)
        top8 = sb1("top8", [128, E], F32)
        dpair = sb1("dpair", [128, 2], F32)
        ep_ = sb1("ep", [128, 2], F32)
        ep1 = sb1("ep1", [128, 2], F32)
        gp = sb1("gp", [128, 2], F32)
        eq2 = sb1("eq2", [128, E], F32)
        t12 = sb1("t12", [128, E], F32)
        mask01 = sb1("mask01", [128, E], F32)
        m1t = sb1("m1t", [128, E], F32)
        lnv = sb1("lnv", [128, 1], F32)
        rstd = sb1("rstd", [128, 1], F32)

        em.op("vector", lambda e: e.memset(running[:, :], 0.0), inc=s_v)
        em.op("vector", lambda e: e.memset(epsln[:, :], EPS_LN), inc=s_v)
        em.wait("scalar", s_v)
        em.op("scalar", lambda e: e.activation(out=wtmp[:, 0:1], in_=epsln[:, :],
              func=AF.Gelu_apprx_tanh))
        em.op("scalar", lambda e: e.activation(out=wtmp[:, 1:2], in_=epsln[:, :], func=AF.Ln))
        em.op("scalar", lambda e: e.activation(out=wtmp[:, 2:3], in_=epsln[:, :], func=AF.Exp),
              inc=s_s)
        warm_s = s_s.n

        dma("sync", g_sync, xbuf[0][:, :], x_h[0:128, :])
        x_loaded = [g_sync.mark(), None]
        xc_done = [0, 0]

        ptr_free = [0, 0]
        xtok_store = [None, None]
        prev_ppos_copy = 0
        prev_plg_copy = 0
        prev_xnTbf_copy = 0
        prev_tr_done = 0
        prev_ptt_read = 0
        prev_xnT_store = None

        for t in range(NT):
            bi = t % 2
            if t + 1 < NT:
                em.wait("sync", s_v, xc_done[(t + 1) % 2])
                dma("sync", g_sync, xbuf[(t + 1) % 2][:, :],
                    x_h[(t + 1) * 128:(t + 2) * 128, :])
                x_loaded[(t + 1) % 2] = g_sync.mark()

            xt = xbuf[bi]
            em.waitm("vector", g_sync, x_loaded[bi])
            em.op("vector", lambda e, a=xt: e.bn_stats(out=bstats[:, 0:6], in_=a[:, 0:512]))
            em.op("vector", lambda e, a=xt: e.bn_stats(out=bstats[:, 6:12], in_=a[:, 512:1024]),
                  inc=s_v)
            em.wait("vector", s_v)
            bs3 = bstats[:, :].rearrange("p (g s) -> p g s", s=3)
            em.op("vector", lambda e, bs3=bs3: e.tensor_scalar(
                  out=gvar[:, :].rearrange("p (g s) -> p g s", s=1), in0=bs3[:, :, 2:3],
                  scalar1=1.0 / 256.0, scalar2=None, op0=ALU.mult))
            em.op("vector", lambda e, bs3=bs3: e.tensor_tensor(
                  out=gmsq[:, :].rearrange("p (g s) -> p g s", s=1),
                  in0=bs3[:, :, 1:2], in1=bs3[:, :, 1:2], op=ALU.mult), inc=s_v)
            em.wait("vector", s_v)
            em.op("vector", lambda e: e.tensor_tensor(out=ex2g[:, :], in0=gvar[:, :],
                  in1=gmsq[:, :], op=ALU.add))
            em.op("vector", lambda e, bs3=bs3: e.reduce_sum(
                  out=msum[:, :],
                  in_=bs3[:, :, 1:2].rearrange("p g s -> p (g s)"), axis=AX.X), inc=s_v)
            em.wait("vector", s_v)
            em.op("vector", lambda e: e.reduce_sum(out=exs[:, :], in_=ex2g[:, :], axis=AX.X))
            em.op("vector", lambda e: e.tensor_scalar(out=meanv[:, :], in0=msum[:, :],
                  scalar1=0.25, scalar2=None, op0=ALU.mult), inc=s_v)
            em.wait("vector", s_v)
            em.op("vector", lambda e: e.tensor_tensor(out=msq2[:, :], in0=meanv[:, :],
                  in1=meanv[:, :], op=ALU.mult), inc=s_v)
            em.wait("vector", s_v)
            em.op("vector", lambda e: e.scalar_tensor_tensor(out=varv[:, :], in0=exs[:, :],
                  scalar=0.25, in1=msq2[:, :], op0=ALU.mult, op1=ALU.subtract), inc=s_v)
            em.wait("scalar", s_v)
            em.wait("scalar", s_s, warm_s)
            em.op("scalar", lambda e: e.activation(out=lnv[:, :], in_=varv[:, :],
                  func=AF.Ln, bias=epsln[:, 0:1]), inc=s_s)
            em.op("scalar", lambda e: e.activation(out=rstd[:, :], in_=lnv[:, :],
                  func=AF.Exp, scale=-0.5), inc=s_s)
            em.wait("vector", s_s)
            em.wait("vector", s_t, prev_tr_done)
            em.op("vector", lambda e, a=xt: e.tensor_scalar(out=xn[:, :], in0=a[:, :],
                  scalar1=meanv[:, 0:1], scalar2=rstd[:, 0:1], op0=ALU.subtract,
                  op1=ALU.mult), inc=s_v)
            xc_done[bi] = s_v.n
            em.waitm("vector", g_gp, xtok_store[bi])
            em.op("vector", lambda e, bi=bi: e.tensor_copy(out=xtokb[bi][:, :],
                  in_=xn[:, :]), inc=s_v)
            em.wait("gpsimd", s_v)
            dma("gpsimd", g_gp, xtok_h[t * 128:(t + 1) * 128, :], xtokb[bi][:, :])
            xtok_store[bi] = g_gp.mark()

            em.wait("tensor", s_v)
            em.waitm("tensor", g_sync, startup_sync)
            pt = ptr[bi]
            for j in range(JC):
                em.op("tensor", lambda e, j=j, pt=pt: e.transpose(
                    out=pt[:, j * 128:(j + 1) * 128],
                    in_=xn[:, j * 128:(j + 1) * 128], identity=ident[:, :]),
                    inc=(s_t if j == JC - 1 else None))
            tr_done = s_t.n
            prev_tr_done = tr_done
            em.wait("vector", s_t, tr_done)
            em.wait("vector", s_s, prev_xnTbf_copy)
            em.op("vector", lambda e, pt=pt: e.tensor_copy(out=xnT32[:, :], in_=pt[:, :]), inc=s_v)
            ptr_free[bi] = s_v.n
            xnT32_done = s_v.n
            em.wait("scalar", s_v, xnT32_done)
            em.waitm("scalar", g_sq, prev_xnT_store)
            em.op("scalar", lambda e: e.copy(out=xnTbf[:, :], in_=xnT32[:, :]), inc=s_s)
            prev_xnTbf_copy = s_s.n
            dma_fn("scalar", g_sq, lambda e, t=t: e.dma_start(
                out=xnT_h[:, :, t * 128:(t + 1) * 128],
                in_=xnTbf[:, :].rearrange("p (j t) -> p j t", j=JC)))
            prev_xnT_store = g_sq.mark()

            em.wait("tensor", s_v, xnT32_done)
            em.wait("tensor", s_v, prev_plg_copy)
            for j in range(JC):
                em.op("tensor", lambda e, j=j: e.matmul(
                    out=plg[:, :], lhsT=xnT32[:, j * 128:(j + 1) * 128],
                    rhs=wrT[:, j * E:(j + 1) * E],
                    start=(j == 0), stop=(j == JC - 1)),
                    inc=(s_t if j == JC - 1 else None))
            plg_mm = s_t.n
            em.wait("vector", s_t, plg_mm)
            em.op("vector", lambda e: e.tensor_copy(out=lgs[:, :], in_=plg[:, :]), inc=s_v)
            prev_plg_copy = s_v.n
            em.wait("vector", s_v)

            em.op("vector", lambda e: e.max(out=top8[:, :], in_=lgs[:, :]), inc=s_v)
            em.wait("vector", s_v)
            em.op("vector", lambda e: e.tensor_tensor(out=dpair[:, 0:1], in0=top8[:, 0:1],
                  in1=top8[:, 1:2], op=ALU.subtract))
            em.op("vector", lambda e: e.tensor_tensor(out=dpair[:, 1:2], in0=top8[:, 1:2],
                  in1=top8[:, 0:1], op=ALU.subtract), inc=s_v)
            em.wait("scalar", s_v)
            em.op("scalar", lambda e: e.activation(out=ep_[:, :], in_=dpair[:, :],
                  func=AF.Exp, scale=-1.0), inc=s_s)
            em.wait("vector", s_s)
            em.op("vector", lambda e: e.tensor_scalar(out=ep1[:, :], in0=ep_[:, :],
                  scalar1=1.0, scalar2=None, op0=ALU.add), inc=s_v)
            em.wait("vector", s_v)
            em.op("vector", lambda e: e.reciprocal(out=gp[:, :], in_=ep1[:, :]), inc=s_v)
            em.wait("vector", s_v)
            sl = slice(t * E, (t + 1) * E)
            em.op("vector", lambda e, sl=sl: e.tensor_scalar(out=eq1_all[:, sl], in0=lgs[:, :],
                  scalar1=top8[:, 0:1], scalar2=None, op0=ALU.is_equal), inc=s_v)
            em.wait("vector", s_v)
            em.op("vector", lambda e: e.tensor_scalar(out=eq2[:, :], in0=lgs[:, :],
                  scalar1=top8[:, 1:2], scalar2=None, op0=ALU.is_equal), inc=s_v)
            em.wait("vector", s_v)
            em.op("vector", lambda e, sl=sl: e.tensor_tensor(out=t12[:, :], in0=eq1_all[:, sl],
                  in1=eq2[:, :], op=ALU.mult), inc=s_v)
            em.wait("vector", s_v)
            em.op("vector", lambda e, sl=sl: e.tensor_tensor(out=eq2p_all[:, sl], in0=eq2[:, :],
                  in1=t12[:, :], op=ALU.subtract), inc=s_v)
            em.wait("vector", s_v)
            em.op("vector", lambda e, sl=sl: e.tensor_tensor(out=mask01[:, :], in0=eq1_all[:, sl],
                  in1=eq2p_all[:, sl], op=ALU.add), inc=s_v)
            mask_done = s_v.n
            em.op("vector", lambda e, sl=sl: e.tensor_scalar(out=m1t[:, :], in0=eq1_all[:, sl],
                  scalar1=gp[:, 0:1], scalar2=None, op0=ALU.mult), inc=s_v)
            em.wait("vector", s_v)
            em.op("vector", lambda e, t=t, sl=sl: e.scalar_tensor_tensor(
                  out=gates_all[:, sl], in0=eq2p_all[:, sl],
                  scalar=gp[:, 1:2], in1=m1t[:, :], op0=ALU.mult, op1=ALU.add), inc=s_v)

            em.wait("tensor", s_v, mask_done)
            em.wait("tensor", s_v, prev_ppos_copy)
            em.op("tensor", lambda e: e.matmul(out=ppos[:, :], lhsT=triu[:, :],
                  rhs=mask01[:, :], start=True, stop=False))
            em.op("tensor", lambda e: e.matmul(out=ppos[:, :], lhsT=onesrow[:, :],
                  rhs=running[0:1, :], start=False, stop=True), inc=s_t)
            pos_mm = s_t.n
            em.op("tensor", lambda e: e.matmul(out=ptt[:, :], lhsT=onescol[:, :],
                  rhs=mask01[:, :], start=True, stop=True), inc=s_t)
            tt_mm = s_t.n
            em.wait("vector", s_t, pos_mm)
            em.op("vector", lambda e, t=t: e.tensor_copy(
                  out=pos_all[:, t * E:(t + 1) * E], in_=ppos[:, :]), inc=s_v)
            prev_ppos_copy = s_v.n
            em.wait("vector", s_t, tt_mm)
            em.op("vector", lambda e: e.tensor_tensor(out=running[:, :], in0=running[:, :],
                  in1=ptt[0:1, :], op=ALU.add), inc=s_v)
        running_done = s_v.n
        p1_v = s_v.n
        p1_s = s_s.n
        p1_t = s_t.n

        em.wait("gpsimd", s_v, running_done)
        dma("gpsimd", g_gp, cci_h, running[0:1, :])
        em.waitm("gpsimd", g_gp)
        em.op("gpsimd", lambda e: e.collective_compute(
            "AllGather", ALU.bypass, replica_groups=[list(range(NCORES))],
            ins=[cci_h], outs=[cco_h]), inc=s_cc)
        em.wait("gpsimd", s_cc)
        dma("gpsimd", g_gp, countsAll[:, :], cco_h)
        counts_mark = g_gp.mark()
        em.waitm("gpsimd", g_gp)
        em.op("gpsimd", lambda e: e.collective_compute(
            "AllGather", ALU.bypass, replica_groups=[list(range(NCORES))],
            ins=[xtok_h], outs=[xall_h]), inc=s_cc)
        es_p1.close()

        poff = PB[0:1, 0:8]
        pcb = PB[:, 512:520]
        pgT = [PB[0:E, 1024:1152], PB[0:E, 1536:1664]]
        em.wait("tensor", s_v)
        em.wait("tensor", s_cc, 1)
        em.waitm("tensor", g_gp, counts_mark)
        em.op("tensor", lambda e: e.matmul(out=poff[:, :], lhsT=cmask[:, :],
              rhs=countsAll[:, :], start=True, stop=True), inc=s_t)
        em.wait("vector", s_t)
        em.op("vector", lambda e: e.tensor_scalar(out=capm[:, :], in0=poff[0:1, :],
              scalar1=-1.0, scalar2=float(CAP), op0=ALU.mult, op1=ALU.add), inc=s_v)
        em.wait("tensor", s_v)
        em.op("tensor", lambda e: e.matmul(out=pcb[:, :], lhsT=onesrow[:, :],
              rhs=capm[0:1, :], start=True, stop=True), inc=s_t)
        em.wait("vector", s_t)
        em.op("vector", lambda e: e.tensor_copy(out=capb[:, :], in_=pcb[:, :]), inc=s_v)
        em.wait("vector", s_v)
        for t in range(NT):
            em.op("vector", lambda e, t=t: e.tensor_tensor(
                  out=kept[:, t * E:(t + 1) * E], in0=pos_all[:, t * E:(t + 1) * E],
                  in1=capb[:, :], op=ALU.is_lt), inc=(s_v if t == NT - 1 else None))
        em.wait("vector", s_v)
        em.op("vector", lambda e: e.tensor_tensor(out=gates_all[:, :], in0=gates_all[:, :],
              in1=kept[:, :], op=ALU.mult), inc=s_v)
        gates_final = s_v.n
        if DEBUG:
            em.wait("sync", s_v, gates_final)
            dma("sync", g_sync, gates_dbg_h, gates_all[:, :])
            dma("sync", g_sync, pos_dbg_h, pos_all[:, :])

        es_p2 = contextlib.ExitStack()
        def sb2(name, shape, dt):
            return es_p2.enter_context(nc.sbuf_tensor(name, shape, dt))
        tmpO = sb2("tmpO", [128, NT * E], F32)
        wrk = sb2("wrk", [128, NT * E], F32)
        sloto1 = sb2("sloto1", [128, NT], F32)
        sloto2 = sb2("sloto2", [128, NT], F32)
        k1a = sb2("k1a", [128, NT], F32)
        k2a = sb2("k2a", [128, NT], F32)
        g1k = sb2("g1k", [128, NT], F32)
        g2k = sb2("g2k", [128, NT], F32)
        er1 = sb2("er1", [128, NT], F32)
        er2 = sb2("er2", [128, NT], F32)
        uf = sb2("uf", [128, NT], F32)
        w32a = sb2("w32a", [128, NT], F32)
        w32b = sb2("w32b", [128, NT], F32)
        TG = sb2("TG", [128, NT * 4], F32)
        idxsc = sb2("idxsc", [128, NT * 2], U32)
        zdt = sb2("zdt", [128, DTR * 2 // 128], F32)

        em.waitm("vector", g_sq)
        em.waitm("vector", g_sync)
        em.op("vector", lambda e: e.tensor_copy(out=offrow[:, :], in_=poff[0:1, :]), inc=s_v)
        em.wait("tensor", s_v)
        em.op("tensor", lambda e: e.matmul(out=pcb[:, :], lhsT=onesrow[:, :],
              rhs=offrow[0:1, :], start=True, stop=True), inc=s_t)
        em.wait("vector", s_t)
        em.op("vector", lambda e: e.tensor_copy(out=capbO[:, :], in_=pcb[:, :]), inc=s_v)
        em.wait("vector", s_v)
        em.op("vector", lambda e: e.tensor_scalar(out=ebc1280[:, :], in0=iota8b[:, :],
              scalar1=float(SPC), scalar2=None, op0=ALU.mult), inc=s_v)
        em.wait("vector", s_v)
        for t in range(NT):
            sl = slice(t * E, (t + 1) * E)
            em.op("vector", lambda e, sl=sl: e.tensor_tensor(out=tmpO[:, sl],
                  in0=pos_all[:, sl], in1=capbO[:, :], op=ALU.add))

        def wred(dst, a, b):
            em.op("vector", lambda e, a=a, b=b: e.tensor_tensor(out=wrk[:, :], in0=a,
                  in1=b, op=ALU.mult), inc=s_v)
            em.wait("vector", s_v)
            em.op("vector", lambda e, dst=dst: e.tensor_reduce(
                  out=dst[:, :].rearrange("p (t o) -> p t o", o=1),
                  in_=wrk[:, :].rearrange("p (t e) -> p t e", e=E),
                  axis=AX.X, op=ALU.add), inc=s_v)
            em.wait("vector", s_v)

        wred(sloto1, tmpO[:, :], eq1_all[:, :])
        wred(sloto2, tmpO[:, :], eq2p_all[:, :])
        wred(k1a, kept[:, :], eq1_all[:, :])
        wred(k2a, kept[:, :], eq2p_all[:, :])
        wred(g1k, gates_all[:, :], eq1_all[:, :])
        wred(g2k, gates_all[:, :], eq2p_all[:, :])
        for dst, eqa in ((er1, eq1_all), (er2, eq2p_all)):
            for t in range(NT):
                sl = slice(t * E, (t + 1) * E)
                em.op("vector", lambda e, sl=sl, eqa=eqa: e.tensor_tensor(out=tmpO[:, sl],
                      in0=eqa[:, sl], in1=ebc1280[:, :], op=ALU.mult))
            em.op("vector", lambda e, dst=dst: e.tensor_reduce(
                  out=dst[:, :].rearrange("p (t o) -> p t o", o=1),
                  in_=tmpO[:, :].rearrange("p (t e) -> p t e", e=E),
                  axis=AX.X, op=ALU.add), inc=s_v)
            em.wait("vector", s_v)

        for a in range(2):
            so = sloto1 if a == 0 else sloto2
            ka = k1a if a == 0 else k2a
            gk = g1k if a == 0 else g2k
            er = er1 if a == 0 else er2
            em.op("vector", lambda e, so=so: e.tensor_scalar(out=uf[:, :], in0=so[:, :],
                  scalar1=float(SPC), scalar2=None, op0=ALU.is_ge), inc=s_v)
            for j in range(2, NCORES):
                em.wait("vector", s_v)
                em.op("vector", lambda e, so=so, j=j: e.tensor_scalar(out=w32b[:, :],
                      in0=so[:, :], scalar1=float(j * SPC), scalar2=None,
                      op0=ALU.is_ge), inc=s_v)
                em.wait("vector", s_v)
                em.op("vector", lambda e: e.tensor_tensor(out=uf[:, :], in0=uf[:, :],
                      in1=w32b[:, :], op=ALU.add), inc=s_v)
            em.wait("vector", s_v)
            em.op("vector", lambda e, so=so: e.tensor_scalar(out=w32a[:, :], in0=uf[:, :],
                  scalar1=float(YR - SPC), scalar2=None, op0=ALU.mult), inc=s_v)
            em.wait("vector", s_v)
            em.op("vector", lambda e, so=so: e.tensor_tensor(out=w32a[:, :], in0=w32a[:, :],
                  in1=so[:, :], op=ALU.add), inc=s_v)
            em.wait("vector", s_v)
            em.op("vector", lambda e, er=er: e.tensor_tensor(out=w32a[:, :], in0=w32a[:, :],
                  in1=er[:, :], op=ALU.add), inc=s_v)
            em.wait("vector", s_v)
            em.op("vector", lambda e, ka=ka: e.tensor_tensor(out=w32a[:, :], in0=w32a[:, :],
                  in1=ka[:, :], op=ALU.mult), inc=s_v)
            em.op("vector", lambda e, ka=ka: e.tensor_scalar(out=w32b[:, :], in0=ka[:, :],
                  scalar1=-float(YR - 1), scalar2=float(YR - 1), op0=ALU.mult,
                  op1=ALU.add), inc=s_v)
            em.wait("vector", s_v)
            em.op("vector", lambda e: e.tensor_tensor(out=w32a[:, :], in0=w32a[:, :],
                  in1=w32b[:, :], op=ALU.add), inc=s_v)
            em.wait("vector", s_v)
            em.op("vector", lambda e, a=a: e.tensor_copy(
                  out=ridx[:, a::2], in_=w32a[:, :]), inc=s_v)
            em.op("vector", lambda e, so=so: e.tensor_scalar(out=w32a[:, :], in0=uf[:, :],
                  scalar1=float(RPR - SPC), scalar2=None, op0=ALU.mult), inc=s_v)
            em.wait("vector", s_v)
            em.op("vector", lambda e, so=so: e.tensor_tensor(out=w32a[:, :], in0=w32a[:, :],
                  in1=so[:, :], op=ALU.add), inc=s_v)
            em.wait("vector", s_v)
            em.op("vector", lambda e, er=er: e.tensor_tensor(out=w32a[:, :], in0=w32a[:, :],
                  in1=er[:, :], op=ALU.add), inc=s_v)
            em.wait("vector", s_v)
            em.op("vector", lambda e, ka=ka: e.tensor_tensor(out=w32a[:, :], in0=w32a[:, :],
                  in1=ka[:, :], op=ALU.mult), inc=s_v)
            em.op("vector", lambda e, ka=ka: e.tensor_scalar(out=w32b[:, :], in0=ka[:, :],
                  scalar1=-1.0, scalar2=1.0, op0=ALU.mult, op1=ALU.add), inc=s_v)
            em.wait("vector", s_v)
            em.op("vector", lambda e: e.tensor_scalar(out=w32b[:, :], in0=w32b[:, :],
                  scalar1=trow[:, 0:1], scalar2=None, op0=ALU.mult), inc=s_v)
            em.wait("vector", s_v)
            em.op("vector", lambda e: e.tensor_tensor(out=w32a[:, :], in0=w32a[:, :],
                  in1=w32b[:, :], op=ALU.add), inc=s_v)
            em.wait("vector", s_v)
            em.op("vector", lambda e, a=a: e.tensor_copy(
                  out=idxsc[:, a::2], in_=w32a[:, :]), inc=s_v)
            em.op("vector", lambda e, a=a: e.tensor_copy(
                  out=TG[:, 2 * a::4], in_=tgrid[:, :]), inc=s_v)
            em.op("vector", lambda e, a=a, gk=gk: e.tensor_copy(
                  out=TG[:, 2 * a + 1::4], in_=gk[:, :]), inc=s_v)
        em.op("vector", lambda e: e.memset(zdt[:, :], 0.0), inc=s_v)
        scat_ready = s_v.n

        em.wait("gpsimd", s_v, scat_ready)
        dma("gpsimd", g_gp, dtab_h.rearrange("(p a) c -> p a c", p=128),
            zdt[:, :].rearrange("p (a c) -> p a c", c=2))
        em.waitm("gpsimd", g_gp)
        for t in range(NT):
            for a in range(2):
                col = 2 * t + a
                sgx = g_gp.nxt()
                em.wait("gpsimd", sgx)
                em.op("gpsimd", lambda e, col=col: e.indirect_dma_start(
                    out=dtab_h,
                    out_offset=bass.IndirectOffsetOnAxis(ap=idxsc[:, col:col + 1], axis=0),
                    in_=TG[:, 2 * col:2 * col + 2], in_offset=None), inc=sgx, k=16)
        em.waitm("gpsimd", g_gp)
        em.op("gpsimd", lambda e: e.collective_compute(
            "AllGather", ALU.bypass, replica_groups=[list(range(NCORES))],
            ins=[dtab_h], outs=[dtag_h]), inc=s_cc)
        em.waitm("vector", g_sync, startup_sync)
        em.op("vector", lambda e: e.tensor_copy(out=cgridu[:, :], in_=cgrid[:, :]), inc=s_v)
        cg_u32 = s_v.n
        em.wait("gpsimd", s_cc, 3)
        em.wait("gpsimd", s_v, cg_u32)
        dslk_sum = 0
        for v in range(NCORES):
            sgx = g_gp.nxt()
            em.wait("gpsimd", sgx)
            if v > 0:
                em.wait("gpsimd", s_v, dslk_sum)
            em.op("gpsimd", lambda e, v=v: e.indirect_dma_start(
                out=dslk_v[:, :], out_offset=None,
                in_=dtag_h.rearrange("(a b) c -> a (b c)", b=128),
                in_offset=bass.IndirectOffsetOnAxis(ap=cgridu[:, 0:1], axis=0),
                element_offset=v * DTR * 2), inc=sgx, k=16)
            vm = g_gp.mark()
            em.waitm("vector", g_gp, vm)
            if v == 0:
                em.op("vector", lambda e: e.tensor_copy(out=dslk_all[:, :],
                      in_=dslk_v[:, :]), inc=s_v)
            else:
                em.op("vector", lambda e: e.tensor_tensor(out=dslk_all[:, :],
                      in0=dslk_all[:, :], in1=dslk_v[:, :], op=ALU.add), inc=s_v)
            em.wait("vector", s_v)
            dslk_sum = s_v.n
        dslk_mark = g_gp.mark()
        es_p2.close()
        es_r.close()

        em.waitm("tensor", g_gp, wqkv_gp)
        em.waitm("vector", g_gp, startup_gp)
        em.waitm("vector", g_sync, startup_sync)
        em.op("vector", lambda e: e.memset(
            Vs[:, :].rearrange("p (h z) -> p h z", h=H)[:, :, 128:129], 1.0), inc=s_v)

        attn_store = [None, None]
        for b in range(BC):
            bt0 = b * L
            em.waitm("scalar", g_sq)
            em.wait("scalar", s_t)
            dma_fn("scalar", g_sq, lambda e, bt0=bt0: e.dma_start(
                out=xnTb[:, :].rearrange("p (j t) -> p j t", j=JC),
                in_=xnT_h[:, :, bt0:bt0 + L]))
            xnTb_ld = g_sq.mark()

            pq = [PA[:, 0:512], PA[:, 512:1024]]
            pq_free = [0, 0]
            em.wait("tensor", s_v)
            em.waitm("tensor", g_sq, xnTb_ld)
            for tt in range(L // 512):
                for dq in range(JC):
                    k = tt * JC + dq
                    pqi = pq[k % 2]
                    em.wait("tensor", s_v, pq_free[k % 2])
                    for j in range(JC):
                        em.op("tensor", lambda e, j=j, dq=dq, tt=tt, pqi=pqi: e.matmul(
                            out=pqi[:, :],
                            lhsT=Wq_sb[:, j * C + dq * 128: j * C + (dq + 1) * 128],
                            rhs=xnTb[:, j * L + tt * 512: j * L + (tt + 1) * 512],
                            start=(j == 0), stop=(j == JC - 1)),
                            inc=(s_t if j == JC - 1 else None))
                    qmm = s_t.n
                    em.wait("vector", s_t, qmm)
                    em.op("vector", lambda e, dq=dq, pqi=pqi: e.tensor_scalar(
                        out=tq1[:, :], in0=pqi[:, :], scalar1=bqT[:, dq:dq + 1],
                        scalar2=0.0, op0=ALU.add, op1=ALU.min), inc=s_v)
                    em.wait("scalar", s_v)
                    em.op("scalar", lambda e: e.activation(out=tq2[:, :], in_=tq1[:, :],
                          func=AF.Exp), inc=s_s)
                    em.wait("vector", s_s)
                    em.op("vector", lambda e, dq=dq, pqi=pqi: e.tensor_scalar(
                        out=tq1[:, :], in0=pqi[:, :], scalar1=bqT[:, dq:dq + 1],
                        scalar2=0.0, op0=ALU.add, op1=ALU.max), inc=s_v)
                    em.wait("vector", s_v)
                    em.op("vector", lambda e, dq=dq, tt=tt: e.tensor_tensor(
                        out=QT[:, dq * L + tt * 512: dq * L + (tt + 1) * 512],
                        in0=tq1[:, :], in1=tq2[:, :], op=ALU.add), inc=s_v)
                    pq_free[k % 2] = s_v.n
            qt_done = s_v.n

            pk = PA[:, 0:1024]
            pv = PA[:, 1024:2048]
            pkv = PB[:, 0:2048]
            em.wait("tensor", s_v)
            em.op("vector", lambda e: e.memset(KVacc[:, :], 0.0), inc=s_v)
            kv_added = s_v.n
            for t in range(L // 128):
                for (W_sb, bb, pdst) in ((Wk_sb, bkb, pk), (Wv_sb, bvb, pv)):
                    for half in range(2):
                        for j in range(JC):
                            em.op("tensor", lambda e, j=j, half=half, t=t, W_sb=W_sb, pdst=pdst: e.matmul(
                                out=pdst[:, half * 512:(half + 1) * 512],
                                lhsT=xnTb[:, j * L + t * 128: j * L + (t + 1) * 128],
                                rhs=W_sb[:, j * C + half * 512: j * C + (half + 1) * 512],
                                start=(j == 0), stop=False))
                        em.op("tensor", lambda e, half=half, bb=bb, pdst=pdst: e.matmul(
                            out=pdst[:, half * 512:(half + 1) * 512],
                            lhsT=onesrowb[:, :],
                            rhs=bb[:, half * 512:(half + 1) * 512],
                            start=False, stop=True),
                            inc=s_t)
                kv_mm = s_t.n
                em.wait("vector", s_t, kv_mm)
                em.op("vector", lambda e: e.tensor_scalar(out=tmka[:, :], in0=pk[:, :],
                      scalar1=0.0, scalar2=None, op0=ALU.min), inc=s_v)
                em.wait("scalar", s_v)
                em.op("scalar", lambda e: e.activation(out=tmkb[:, :], in_=tmka[:, :],
                      func=AF.Exp), inc=s_s)
                em.wait("vector", s_s)
                em.op("vector", lambda e: e.scalar_tensor_tensor(out=Kb[:, :], in0=pk[:, :],
                      scalar=0.0, in1=tmkb[:, :], op0=ALU.max, op1=ALU.add), inc=s_v)
                em.op("vector", lambda e: e.tensor_scalar(
                      out=Vs[:, :].rearrange("p (h z) -> p h z", h=H)[:, :, 0:128],
                      in0=pv[:, :].rearrange("p (h d) -> p h d", h=H),
                      scalar1=1.0 / L, scalar2=None, op0=ALU.mult), inc=s_v)
                kvin_done = s_v.n
                em.wait("tensor", s_v, kvin_done)
                em.wait("tensor", s_v, kv_added)
                for h in range(H):
                    em.op("tensor", lambda e, h=h, t=t: e.matmul(
                        out=pkv[:, h * 256: h * 256 + 129],
                        lhsT=Kb[:, h * 128:(h + 1) * 128],
                        rhs=Vs[:, h * 132: h * 132 + 129],
                        start=True, stop=True),
                        inc=(s_t if h == H - 1 else None))
                kv_mm2 = s_t.n
                em.wait("vector", s_t, kv_mm2)
                em.op("vector", lambda e: e.tensor_tensor(
                    out=KVacc[:, :].rearrange("p (h z) -> p h z", h=H)[:, :, 0:129],
                    in0=KVacc[:, :].rearrange("p (h z) -> p h z", h=H)[:, :, 0:129],
                    in1=pkv[:, :].rearrange("p (h z) -> p h z", h=H)[:, :, 0:129],
                    op=ALU.add), inc=s_v)
                kv_added = s_v.n
            em.wait("vector", s_v, kv_added)
            em.op("vector", lambda e: e.tensor_copy(
                out=KVs[:, :].rearrange("p (h z) -> p h z", h=H)[:, :, 0:129],
                in_=KVacc[:, :].rearrange("p (h z) -> p h z", h=H)[:, :, 0:129]), inc=s_v)
            kvs_done = s_v.n

            pao = [PA[:, 0:2048], PB[:, 0:2048]]
            pao_free = [0, 0]
            em.wait("tensor", s_v, kvs_done)
            for t in range(L // 128):
                bi = t % 2
                pa = pao[bi]
                em.wait("tensor", s_v, pao_free[bi])
                for h in range(H):
                    em.op("tensor", lambda e, h=h, t=t, pa=pa: e.matmul(
                        out=pa[:, h * 256: h * 256 + 129],
                        lhsT=QT[:, h * L + t * 128: h * L + (t + 1) * 128],
                        rhs=KVs[:, h * 132: h * 132 + 129],
                        start=True, stop=True),
                        inc=(s_t if h == H - 1 else None))
                ao_mm = s_t.n
                em.wait("vector", s_t, ao_mm)
                em.op("vector", lambda e, pa=pa: e.tensor_scalar(
                      out=zden[:, :].rearrange("p (h o) -> p h o", o=1),
                      in0=pa[:, :].rearrange("p (h z) -> p h z", h=H)[:, :, 128:129],
                      scalar1=EPS_ATT, scalar2=None, op0=ALU.add), inc=s_v)
                em.wait("vector", s_v)
                em.op("vector", lambda e: e.reciprocal(out=zrec[:, :], in_=zden[:, :]), inc=s_v)
                em.wait("vector", s_v)
                ab = attnbuf[bi]
                em.waitm("vector", g_sync, attn_store[bi])
                for h in range(H):
                    em.op("vector", lambda e, h=h, pa=pa, ab=ab: e.tensor_scalar(
                          out=ab[:, h * 128:(h + 1) * 128],
                          in0=pa[:, h * 256: h * 256 + 128],
                          scalar1=zrec[:, h:h + 1], scalar2=float(L),
                          op0=ALU.mult, op1=ALU.mult),
                          inc=(s_v if h == H - 1 else None))
                pao_free[bi] = s_v.n
                em.wait("sync", s_v)
                dma_fn("sync", g_sync, lambda e, t=t, bt0=bt0, ab=ab: e.dma_start(
                    out=attn_h[bt0 + t * 128: bt0 + (t + 1) * 128, :], in_=ab[:, :]))
                attn_store[bi] = g_sync.mark()
        attn_all_stored = g_sync.mark()
        p3_t = s_t.n
        es_att.close()
        es_p4 = contextlib.ExitStack()
        w1_sb = es_p4.enter_context(nc.sbuf_tensor("w1_sb", [128, JC * FF], BF16))
        w2_sb = es_p4.enter_context(nc.sbuf_tensor("w2_sb", [128, FCN * C], BF16))
        h_sb = es_p4.enter_context(nc.sbuf_tensor("h_sb", [128, FCN * 512], BF16))
        xg1 = [es_p4.enter_context(nc.sbuf_tensor("xg10", [128, C], F32)),
               es_p4.enter_context(nc.sbuf_tensor("xg11", [128, C], F32))]
        xgT = es_p4.enter_context(nc.sbuf_tensor("xgT0", [128, JC * 512], BF16))
        yw = [es_p4.enter_context(nc.sbuf_tensor("yw0", [128, C], BF16)),
              es_p4.enter_context(nc.sbuf_tensor("yw1", [128, C], BF16))]
        b2bc = es_p4.enter_context(nc.sbuf_tensor("b2bc", [128, C], BF16))
        idxf4 = es_p4.enter_context(nc.sbuf_tensor("idxf4", [128, E * (SPC // 128)], F32))
        idxu4 = es_p4.enter_context(nc.sbuf_tensor("idxu4", [128, E * (SPC // 128)], U32))
        gts4 = es_p4.enter_context(nc.sbuf_tensor("gts4", [128, E * (SPC // 128)], F32))
        ph = [PA[:, 0:512], PA[:, 512:1024]]
        pst = [PA[:, 1024:1536], PA[:, 1536:2048]]
        py = [PB[:, 0:1024], PB[:, 1024:2048]]
        NKT = SPC // 128

        def load_w1(e_idx):
            src = w1_h[e_idx].rearrange("(j p) f -> p j f", p=128)
            dst = w1_sb[:, :].rearrange("p (j f) -> p j f", j=JC)
            dma("gpsimd", g_gp, dst, src)

        def load_w2(e_idx):
            src = w2_h[e_idx].rearrange("(f p) d -> p f d", p=128)
            dst = w2_sb[:, :].rearrange("p (f d) -> p f d", f=FCN)
            dma("gpsimd", g_gp, dst, src)
            dma("gpsimd", g_gp, b2bc[:, :],
                b2row_h[0:1, e_idx * C:(e_idx + 1) * C].broadcast_to([128, C]))

        em.waitm("vector", g_sync)
        em.waitm("vector", g_sq)
        em.op("vector", lambda e: e.memset(h_sb[:, 0:4096], 0.0), inc=s_v)
        em.wait("gpsimd", s_v)
        dma("gpsimd", g_gp, yslots_h[E * SPC:YR, :].rearrange("(q p) c -> p q c", p=128),
            h_sb[:, 0:4096].rearrange("p (q c) -> p q c", q=(YR - E * SPC) // 128))
        yw_trash_mark = g_gp.mark()
        em.waitm("scalar", g_gp, yw_trash_mark)

        em.wait("gpsimd", s_t, p3_t)
        em.wait("gpsimd", s_v)
        em.waitm("gpsimd", g_sync)
        em.waitm("gpsimd", g_sq)
        load_w1(0)
        load_w2(0)
        w12_ld = g_gp.mark()
        em.waitm("scalar", g_sync, startup_sync)
        em.op("scalar", lambda e: e.activation(out=wtmp[:, 3:4], in_=epsln[:, :],
              func=AF.Gelu_apprx_tanh), inc=s_s)

        em.wait("tensor", s_v)
        em.op("tensor", lambda e: e.transpose(
            out=pst[0][:, 0:E * NKT], in_=dslk_all[0:E * NKT, 0:256:2],
            identity=ident[0:E * NKT, 0:E * NKT]))
        em.op("tensor", lambda e: e.transpose(
            out=pst[1][:, 0:E * NKT], in_=dslk_all[0:E * NKT, 1:256:2],
            identity=ident[0:E * NKT, 0:E * NKT]), inc=s_t)
        dtr_done = s_t.n
        em.wait("vector", s_t, dtr_done)
        em.op("vector", lambda e: e.tensor_copy(out=idxf4[:, :], in_=pst[0][:, 0:E * NKT]), inc=s_v)
        em.wait("vector", s_v)
        em.op("vector", lambda e: e.tensor_copy(out=gts4[:, :], in_=pst[1][:, 0:E * NKT]), inc=s_v)
        em.wait("vector", s_v)
        em.op("vector", lambda e: e.tensor_copy(out=idxu4[:, :], in_=idxf4[:, :]), inc=s_v)
        pst_free = [s_v.n, s_v.n]
        idx_u32 = s_v.n

        xg_ld = [None, None]
        xg_free = [0, 0]
        xgT_free = [0]
        ph_free = [0, 0]
        py_free = [0, 0]
        yw_free = [None, None]
        h_consumed = 0
        idx_ready = 0
        warm_done = False

        for e_idx in range(E):

            for kt in range(NKT // 4 + (1 if NKT % 4 else 0)):
                g0 = kt * 4
                ng = min(4, NKT - g0)
                NS = ng * 128
                bi = kt % 2
                em.wait("gpsimd", s_v, idx_u32)
                em.wait("gpsimd", s_cc, 2)
                em.wait("tensor", s_t, xgT_free[0])
                for g in range(ng):
                    gb = g % 2
                    em.wait("gpsimd", s_t, xg_free[gb])
                    sgx = g_gp.nxt()
                    em.wait("gpsimd", sgx)
                    em.op("gpsimd", lambda e, g=g, g0=g0, gb=gb, e_idx=e_idx: e.indirect_dma_start(
                        out=xg1[gb][:, :], out_offset=None,
                        in_=xall_h,
                        in_offset=bass.IndirectOffsetOnAxis(
                            ap=idxu4[:, e_idx * NKT + g0 + g:e_idx * NKT + g0 + g + 1],
                            axis=0)), inc=sgx, k=16)
                    xg_ld[gb] = g_gp.mark()
                    em.waitm("tensor", g_gp, xg_ld[gb])
                    for jh in range(2):
                        pb = jh
                        em.wait("tensor", s_v, pst_free[pb])
                        for jj in range(4):
                            j = jh * 4 + jj
                            em.op("tensor", lambda e, j=j, jj=jj, gb=gb, pb=pb: e.transpose(
                                out=pst[pb][:, jj * 128:(jj + 1) * 128],
                                in_=xg1[gb][:, j * 128:(j + 1) * 128],
                                identity=ident[:, :]),
                                inc=(s_t if jj == 3 else None))
                        trj = s_t.n
                        em.wait("vector", s_t, trj)
                        em.op("vector", lambda e, jh=jh, g=g, pb=pb: e.tensor_copy(
                            out=xgT[:, :].rearrange("p (j s) -> p j s", j=JC)
                                [:, jh * 4:(jh + 1) * 4, g * 128:(g + 1) * 128],
                            in_=pst[pb][:, 0:512].rearrange("p (j s) -> p j s", j=4)),
                            inc=s_v)
                        pst_free[pb] = s_v.n
                    xg_free[gb] = s_t.n
                xgT_ready = s_v.n

                if e_idx == 0 and kt == 0 and not warm_done:
                    warm_done = True
                    em.waitm("tensor", g_gp, w12_ld)
                    em.wait("tensor", s_v, xgT_ready)
                    for j in range(JC):
                        em.op("tensor", lambda e, j=j: e.matmul(
                            out=ph[0][:, 0:NS],
                            lhsT=w1_sb[:, j * FF: j * FF + 128],
                            rhs=xgT[:, j * 512: j * 512 + NS],
                            start=(j == 0), stop=(j == JC - 1)),
                            inc=(s_t if j == JC - 1 else None))

                em.wait("tensor", s_t, h_consumed)
                em.waitm("tensor", g_gp, w12_ld)
                em.wait("tensor", s_v, xgT_ready)
                for fc in range(FCN):
                    phi = ph[fc % 2]
                    em.wait("tensor", s_s, ph_free[fc % 2])
                    for j in range(JC):
                        em.op("tensor", lambda e, j=j, fc=fc, phi=phi, NS=NS: e.matmul(
                            out=phi[:, 0:NS],
                            lhsT=w1_sb[:, j * FF + fc * 128: j * FF + (fc + 1) * 128],
                            rhs=xgT[:, j * 512: j * 512 + NS],
                            start=(j == 0), stop=(j == JC - 1)),
                            inc=(s_t if j == JC - 1 else None))
                    hmm = s_t.n
                    em.wait("scalar", s_t, hmm)
                    em.op("scalar", lambda e, fc=fc, e_idx=e_idx, phi=phi, NS=NS: e.activation(
                        out=h_sb[:, fc * 512: fc * 512 + NS], in_=phi[:, 0:NS],
                        func=AF.Gelu_apprx_tanh,
                        bias=b1T[:, e_idx * FCN + fc: e_idx * FCN + fc + 1],
                        scale=1.0), inc=s_s)
                    ph_free[fc % 2] = s_s.n
                xgT_free[0] = s_t.n
                h_done = s_s.n
                if kt == (NKT + 3) // 4 - 1 and e_idx + 1 < E:
                    em.wait("gpsimd", s_t)
                    load_w1(e_idx + 1)

                em.wait("tensor", s_s, h_done)
                for g in range(ng):
                    yb = g % 2
                    em.wait("tensor", s_v, py_free[yb])
                    for dh in range(2):
                        for fc in range(FCN):
                            em.op("tensor", lambda e, g=g, dh=dh, fc=fc, yb=yb: e.matmul(
                                out=py[yb][:, dh * 512:(dh + 1) * 512],
                                lhsT=h_sb[:, fc * 512 + g * 128: fc * 512 + (g + 1) * 128],
                                rhs=w2_sb[:, fc * C + dh * 512: fc * C + (dh + 1) * 512],
                                start=(fc == 0), stop=(fc == FCN - 1)),
                                inc=(s_t if (fc == FCN - 1 and dh == 1) else None))
                    ymm = s_t.n
                    em.wait("vector", s_t, ymm)
                    em.waitm("vector", g_gp, w12_ld)
                    em.waitm("vector", g_gp, yw_free[yb])
                    em.op("vector", lambda e, yb=yb: e.tensor_tensor(
                        out=py[yb][:, :], in0=py[yb][:, :], in1=b2bc[:, :],
                        op=ALU.add), inc=s_v)
                    em.wait("vector", s_v)
                    em.op("vector", lambda e, g=g, g0=g0, yb=yb, e_idx=e_idx: e.tensor_scalar(
                        out=yw[yb][:, :], in0=py[yb][:, :],
                        scalar1=gts4[:, e_idx * NKT + g0 + g:e_idx * NKT + g0 + g + 1],
                        scalar2=None, op0=ALU.mult), inc=s_v)
                    py_free[yb] = s_v.n
                    em.wait("gpsimd", s_v)
                    row0 = e_idx * SPC + (g0 + g) * 128
                    dma("gpsimd", g_gp, yslots_h[row0:row0 + 128, :], yw[yb][:, :])
                    yw_free[yb] = g_gp.mark()
                h_consumed = s_t.n
            idx_ready = s_v.n
            if e_idx + 1 < E:
                em.wait("gpsimd", s_t)
                em.wait("gpsimd", s_v)
                load_w2(e_idx + 1)
                w12_ld = g_gp.mark()
        ffn_all = g_gp.mark()
        p4_t = s_t.n
        es_p4.close()

        em.waitm("gpsimd", g_gp, ffn_all)
        em.op("gpsimd", lambda e: e.collective_compute(
            "AllGather", ALU.bypass, replica_groups=[list(range(NCORES))],
            ins=[yslots_h], outs=[ytall_h]), inc=s_cc)
        es_p5 = contextlib.ExitStack()
        att_t = [es_p5.enter_context(nc.sbuf_tensor("att0", [128, C], F32)),
                 es_p5.enter_context(nc.sbuf_tensor("att1", [128, C], F32))]
        c1b = [es_p5.enter_context(nc.sbuf_tensor("c1b0", [128, C], F32)),
               es_p5.enter_context(nc.sbuf_tensor("c1b1", [128, C], F32))]
        c2b = [es_p5.enter_context(nc.sbuf_tensor("c2b0", [128, C], F32)),
               es_p5.enter_context(nc.sbuf_tensor("c2b1", [128, C], F32))]
        osb = [es_p5.enter_context(nc.sbuf_tensor("osb0", [128, C], F32)),
               es_p5.enter_context(nc.sbuf_tensor("osb1", [128, C], F32))]

        em.waitm("sync", g_sync)
        em.waitm("sync", g_gp)
        em.wait("sync", s_t, p4_t)
        em.wait("sync", s_v)
        em.wait("gpsimd", s_cc, 4)
        ld_done = [None, None]
        gat_done = [None, None]
        comb_done = [0, 0]
        osb_free = [None, None]
        for t in range(NT):
            bi = t % 2
            em.waitm("sync", g_sync, ld_done[bi])
            em.wait("sync", s_v, comb_done[bi])
            dma("sync", g_sync, att_t[bi][:, :], attn_h[t * 128:(t + 1) * 128, :])
            dma("sync", g_sync, xbuf[bi][:, :], x_h[t * 128:(t + 1) * 128, :])
            ld_done[bi] = g_sync.mark()
            em.wait("gpsimd", s_v, comb_done[bi])
            for a in range(2):
                sgx = g_gp.nxt()
                em.wait("gpsimd", sgx)
                em.op("gpsimd", lambda e, t=t, a=a, bi=bi: e.indirect_dma_start(
                    out=(c1b if a == 0 else c2b)[bi][:, :], out_offset=None,
                    in_=ytall_h,
                    in_offset=bass.IndirectOffsetOnAxis(
                        ap=ridx[:, 2 * t + a:2 * t + a + 1], axis=0)), inc=sgx, k=16)
            gat_done[bi] = g_gp.mark()
            em.waitm("vector", g_sync, ld_done[bi])
            em.waitm("vector", g_gp, gat_done[bi])
            em.waitm("vector", g_sync, osb_free[bi])
            em.op("vector", lambda e, bi=bi: e.tensor_tensor(out=osb[bi][:, :],
                  in0=xbuf[bi][:, :], in1=att_t[bi][:, :], op=ALU.add), inc=s_v)
            em.wait("vector", s_v)
            em.op("vector", lambda e, bi=bi: e.tensor_tensor(out=osb[bi][:, :],
                  in0=osb[bi][:, :], in1=c1b[bi][:, :], op=ALU.add), inc=s_v)
            em.wait("vector", s_v)
            em.op("vector", lambda e, bi=bi: e.tensor_tensor(out=osb[bi][:, :],
                  in0=osb[bi][:, :], in1=c2b[bi][:, :], op=ALU.add), inc=s_v)
            comb_done[bi] = s_v.n
            em.wait("sync", s_v, comb_done[bi])
            dma_fn("sync", g_sync, lambda e, t=t, bi=bi: e.dma_start(
                out=y_h[t * 128:(t + 1) * 128, :], in_=osb[bi][:, :]))
            osb_free[bi] = g_sync.mark()
        es_p5.close()


        if DEBUG:
            em.wait("sync", s_cc, 4)
            em.wait("sync", s_v)
            dma("sync", g_sync, dt_dbg_h, dslk_all[:, :])
            dma("sync", g_sync, ys_dbg_h, yslots_h)

        em.waitm("gpsimd", g_sync)
        em.waitm("gpsimd", g_sq)
        em.waitm("gpsimd", g_gp)

        block = es.enter_context(nc.Block())
        em.replay(block)
    return nc


def _per_core_inputs(inp):
    x = np.asarray(inp["L0_attn_x"], dtype=np.float32).reshape(16, L, C)
    wq = np.ascontiguousarray(np.asarray(inp["L0_attn_Wq"], dtype=np.float32))
    wk = np.ascontiguousarray(np.asarray(inp["L0_attn_Wk"], dtype=np.float32))
    wv = np.ascontiguousarray(np.asarray(inp["L0_attn_Wv"], dtype=np.float32))
    bq = np.asarray(inp["L0_attn_bq"], dtype=np.float32)
    bk = np.asarray(inp["L0_attn_bk"], dtype=np.float32)
    bv = np.asarray(inp["L0_attn_bv"], dtype=np.float32)
    wr = np.asarray(inp["L0_ffn_wr"], dtype=np.float32)
    w1 = np.ascontiguousarray(np.asarray(inp["L0_ffn_w1"], dtype=np.float32))
    b1 = np.asarray(inp["L0_ffn_b1"], dtype=np.float32)
    w2 = np.ascontiguousarray(np.asarray(inp["L0_ffn_w2"], dtype=np.float32))
    b2 = np.asarray(inp["L0_ffn_b2"], dtype=np.float32)

    bqT = np.ascontiguousarray(bq.reshape(JC, 128).T)
    wrT = np.ascontiguousarray(wr.T.reshape(JC, 128, E).transpose(1, 0, 2).reshape(128, JC * E))
    b1T = np.ascontiguousarray(b1.reshape(E, FCN, 128).transpose(2, 0, 1).reshape(128, E * FCN))
    b2row = np.ascontiguousarray(b2.reshape(1, E * C))
    ident = np.eye(128, dtype=np.float32)
    triu = np.triu(np.ones((128, 128), dtype=np.float32), 1)
    onesrow = np.ones((1, 128), dtype=np.float32)
    onescol = np.ones((128, 1), dtype=np.float32)
    iota8b = np.tile(np.arange(E, dtype=np.float32), (128, 1))
    pcol = np.arange(128, dtype=np.float32).reshape(128, 1)
    prange = np.arange(128)
    trow = ((prange % NCORES) * RPR + E * SPC + prange // NCORES).astype(np.float32).reshape(128, 1)
    NKT = SPC // 128
    cg = np.zeros(128, np.float32)
    for p in range(E * NKT):
        cg[p] = (p // NKT) * (SPC // 128) * 0 + p
    for p in range(128):
        if p < E * NKT:
            e_, k_ = p // NKT, p % NKT
            cg[p] = e_ * (SPC // 128) + k_
        else:
            cg[p] = 0
    cgrid = cg.reshape(128, 1)

    shared = {"wq": wq, "wk": wk, "wv": wv, "bqT_d": bqT, "bk": bk.reshape(1, C),
              "bv": bv.reshape(1, C), "wrT_d": wrT, "b1T_d": b1T, "b2row_d": b2row,
              "w1": w1, "w2": w2, "ident_d": ident, "triu_d": triu,
              "onesrow_d": onesrow, "onesrowb_d": onesrow, "onescol_d": onescol,
              "iota8b_d": iota8b, "pcol_d": pcol, "trow_d": trow}
    maps = []
    for i in range(NCORES):
        m = dict(shared)
        m["x"] = np.ascontiguousarray(x[BC * i: BC * (i + 1)].reshape(TK, C))
        m["cmask_d"] = (np.arange(NCORES) < i).astype(np.float32).reshape(NCORES, 1)
        m["cgrid_d"] = np.ascontiguousarray(cgrid + i * (RPR // 128))
        m["tgrid_d"] = np.ascontiguousarray(
            (np.arange(NT, dtype=np.float32)[None, :] * 128
             + np.arange(128, dtype=np.float32)[:, None] + TK * i))
        maps.append(m)
    return maps


def kernel(**inputs):
    nc = build()
    in_maps = _per_core_inputs(inputs)
    res = run_bass_kernel_spmd(nc, in_maps, core_ids=list(range(NCORES)))
    out = np.concatenate([np.asarray(r["y"]) for r in res.results], axis=0)
    return out.reshape(16, L, C)
